# version 01
# baseline (speedup 1.0000x reference)
import base64
import contextlib
import numpy as np
import ml_dtypes

import concourse.bass as bass
import concourse.mybir as mybir
from concourse.bass_utils import run_bass_kernel_spmd

F32 = mybir.dt.float32
BF16 = mybir.dt.bfloat16
ALU = mybir.AluOpType
ACT = mybir.ActivationFunctionType
AXX = mybir.AxisListType.X

S, D = 2048, 1024
NH, HD = 16, 64
CH = 128
GT = 256
NG = S // GT
BT = 512
SGT = 1024
EPS_LN, EPS_RMS = 1e-5, 1e-6

_EXPTAB_B64 = "AACAP4Bz/D6Y83g+HYD1Pd8Ycj21ve48d25rPPEq6DsI82Q7isbhOkilXjoqj9s5/oNYOaGD1TjcjVI4pqLPN8zBTDcp68k2mB5HNupbxDUUo0E16PO+NEJOPDT0sbkzAh83MxuVtDJNFDIyS5yvMQwtLTGHxqowcWgoMNYSpi9wxSMvNoChLh5DHy7lDZ0tlOAaLeu6mCzgnBYsa4aUK013EiuPb5Aq9G4OKnR1jCkIgwopc5eIKK+yBijE1IQnWP0CJ3YsgSYKxP4l3zt7JXrA9yQzUXQkHe7wIwyXbSPVS+oiiQxnIorY4yHpr2AhfZLdICCAWiDfeNcfKHxUHwuK0R5iok4eCMXLHQzySB3jKMYcnWlDHBa0wBsuCD4b72W7GtrMOBr+PLYZOrYzGW04sRijwy4YZVesF8DzKReWmKcWyUUlFmP7ohX2uCAVjn6eFA5MHBRaIZoTff4XEw/jlRIbzxMSh8KRETm9DxEXv40QB8gLEDbYiQ//7gcPkQyGDtIwBA6tW4INCI0ADZqJfQzKBfoLdI52C2wj8wr8xG8KCnLsCeMqaQlf7+UIVL9iCJia3wcFgVwHcnLZBrhuVgayddMFoIdQBY2jzQS8yUoECfrHA1A0RQNueMICQMY/AqQdvQF2fjoBl+i3AAAAAAAAAAAAAAAAAAAAAAAAAAAAAACAP5hFGz+yWrw+PHxkPlWVCj4uHKg9hu1LPYFg9zyuCpY8ggI2PP/J3DtT6oU7kHIiOwwPxTpdC2863fwQOgjhrzkgWlU5kWcBOcX5nDjOaz44E/7mN6EajDds9Ck3YirONmoXejYqsBc2zAG4NTg2XzWCYgc15TqkNJw4RzTeqvEzJ5SSMxjPMTN0sdcyFNOCMryyHjK4gsAx9oZpMTOkDTHY0aswhm1QMPPV/C9GWpkvsgY6Lz+p4S7S3oguPAgmLj9oyS29UXQt6C8ULZXCsyxdD1oslUIELIlwoCuBn0Ir9xbsKhYyjyqBtC0qBrfSKTKcfykPCRspQxG8KCgjZChOXwoopNqnJwaeSycPAPcmL9CVJo27NSbrc9wlHbaFJTozIiU5wsQkK65uJFfEECR2nK8j8gZVIx41ASOSvJwikCE+IgWk5iEC5IshKrIpIQLazSDqtXkgB3UXIA+6tx8y314fui0HH976ox7x6kYepkzxHQJbkh3HiTEdXl3XHBSgghzddB4cqjfAG+sraRv6bA0b3I6rGkQcUBphc/wZfR6ZGSu+ORlFUeEYdamIGILHJRi6GckXffJzFyP2ExeAfLMWWrpZFgUPBBb8MaAVoVNCFey66xRC+o4UyXAtFOBk0hOLOH8TnswaE/HHuxI3ymMSXCkKEjOZpxGkTksRAACAP6DCMz+Ac/w+sESxPpjzeD6Yzy4+HYD1PTdjrD3fGHI9c/8pPbW97jwopKc8d25rPDxRJTzxKug7igajOwjzZDv/wyA7isbhOnSJnjpIpV4611YcOiqP2zkCLJo5/oNYOeMIGDmhg9U4We2VONyNUjhK2RM4pqLPN5vMkTfMwUw3KccPNynryTbtyI02mB5HNsPRCzbqW8Q1k+GJNRSjQTVD+Ac16PO+NLQVhjRCTjw03TkENPSxuTOoZIIzAh83M+OVADMblbQyEpt9Mk0UMjIjF/oxS5yvMZCfdjEMLS0xaTTzMIfGqjBF1W8wcWgoMBmC7C/WEqYv1jppL3DFIy8a/+UuNoChLvPOYi4eQx8uAKrfLeUNnS02kFwtlOAaLYiB2Szrupgsmn1WLOCcFix4hNMra4aUK8qVUCtNdxIrhLHNKo9vkCqb10oq9G4OKrYHyCl0dYwp5kFFKQiDCinUhcIoc5eIKHbTPyivsgYoqyq9J8TUhCd+izonWP0CJ3H1tyZ2LIEmkWg1JgrE/iW+5LIl3zt7JdlpMCV6wPck7PetJDNRdCSBjiskHe7wI6UtqSMMl20jOtUmI9VL6iIihaQiiQxnImg9IiKK2OMhnf2fIemvYCHOxR0hfZLdIN2VmyAggFogsG0ZIN941x9RTZcfKHxUH1c0FR8LitEeAACAP31fRz+YRRs/B9rxPrJavD7CsJI+PHxkPszxMT5VlQo+jNvXPS4cqD2c7II9hu1LPbTRHj2BYPc8SqjAPK4KljyItGk8ggI2PNe/DTz/ydw7YPOrO1PqhTs0llA7kHIiO0sH/ToMD8U6M3iZOl0Lbzr/Kjo63fwQOknV4TkI4a85iPmIOSBaVTmjKCY5kWcBOY2PyTjF+Zw4a4F0OM5rPjjUTBQ4E/7mN6rlszehGow36zlaN2z0KTdkXAQ3YirONtiPoDZqF3o2fMVCNiqwFzYKRew1zAG4NQhOjzU4Nl81aNYtNYJiBzUl4NI05TqkNBTOfzScOEc0UCcbNN6q8TP3NbwzJ5SSM61PZDMYzzEzT3oKM3Sx1zJm+6cyFNOCMsLFSzK8sh4yQzD3MbiCwDFr7ZUx9oZpMQTfNTEzpA0x8Z7cMNjRqzA20IUwhm1QMOJSIjDz1fwvn+jEL0ZamS/A3G4vsgY6L5fgEC8/qeEuvL6vLtLeiC6FMFUuPAgmLlVOAS4/aMktKducLb1RdC2sRj4t6C8ULQjR5iyVwrMsT/+LLF0PWixI0ykslUIELC4CziuJcKArpeZ5K4GfQiuWkhcr9xbsKurdtyoWMo8qsQpfKoG0LSobSAcqBrfSKd4apCkynH8pwhFHKQ8JGym+e/EoQxG8KJF3kigoI2QoAACAPwCFVj+gwjM/FqIWP4Bz/D60i9M+sESxPoOLlD6Y83g++JxQPpjPLj5bfBI+HYD1PaS4zT03Y6w9gHSQPd8Ycj2R3ko9c/8pPdpzDj21ve48ng7IPCikpzxPeow8d25rPKVIRTw8USU8yYcKPPEq6DuHjMI7igajOyyciDsI82Q7GNo/O//DIDtbtwY7isbhOkAxvTp0iZ46R9mEOkilXjrakTo611YcOs8BAzoqj9s5vPu3OQIsmjnpMIE5/oNYOdBuNTnjCBg50Mz+OKGD1Tjz6rI4We2VOJZEezjcjVI4AnAwOErZEzjkyPc3pqLPN9T9rTebzJE3jll0N8zBTDdflCs3KccPN2n28DYp68k2eTOpNu3IjTY7n202mB5HNgTbJjbD0Qs29lPqNepbxDXiiqQ1k+GJNWIUZzUUo0E16kIiNUP4BzVV4OM06PO+NBYDoDS0FYY0prdgNEJOPDQ9yx003TkENB+a3TP0sbkzOZubM6hkgjPCh1ozAh83MwxzGTPjlQAzMIDXMhuVtDJ0UpcyEpt9Ml+DVDJNFDIyezkVMiMX+jFCkdExS5yvMeEnkzGQn3YxgKlOMQwtLTGeHRExaTTzMCbMyzCHxqowrBqPMEXVbzDe+EgwcWgoMMoeDTAZguwvnS/GL9YSpi8FKosv1jppL1hwQy9wxSMvAACAP1HrYT99X0c/MfIvP5hFGz/lBgk/B9rxPvBu1T6yWrw+2TimPsKwkj4xdIE+PHxkPjejST7M8TE+FgkdPlWVCj5GmfQ9jNvXPWJ+vj0uHKg9TFuUPZzsgj2cFGc9hu1LPTf3Mz200R49TCgMPYFg9zw1T9o8SqjAPAAFqjyuCpY8TmmEPIi0aTx/Pk48ggI2PIKfIDzXvw080C/6O//J3Dt92MI7YPOrO/a+lztT6oU7F1xsOzSWUDu/Ezg7kHIiOwNcDztLB/06/0vfOgwPxTpd5606M3iZOrdvhzpdC286uPRSOv8qOjrrSiQ63fwQOgnn/zlJ1eE5DEzHOQjhrzlzNps5iPmIOXHCcTkgWlU5VEg8OaMoJjl0ohI5kWcBOfNl5DiNj8k4cuCxOMX5nDjRh4o4a4F0OIDGVzjOaz44yQsoONRMFDjY3wI4E/7mN6PZyzeq5bM3N8KeN6EajDdhSHc36zlaN4CVQDds9Ck3DPwVN2RcBDe+nek2YirONsLwtTbYj6A2BLKNNmoXejZ3tFw2fMVCNpviKzYqsBc2RN0FNgpF7DXcgdA1zAG4NbhiojUITo81nu58NTg2XzXV+0Q1aNYtNTxpGTWCYgc1DfTuNCXg0jTYGLo05TqkNLnukDQUzn80Q79hNJw4RzTizy80UCcbNCzsCDTeqvEzAACAPwFYaj8AhVY/UF9EP6DCMz+yjSQ/FqIWP/DjCT+Ac/w+XBjnPrSL0z5spsE+sESxPrpFoj6Di5Q+lvqHPpjzeD4+5GM++JxQPjD3Pj6Yzy4+2QUgPlt8Ej4FGAY+HYD1PX674D2kuM09fFG8PTdjrD30zZ09gHSQPSY8hD3fGHI9851dPZHeSj0stTk9c/8pPe6dGz3acw4932YCPbW97jx7i9o8ng7IPCAitzwopKc8rHWZPE96jDwamIA8d25rPOaDVzylSEU8OJg0PDxRJTwUVRc8yYcKPHif/TvxKug7FYfUO4eMwjtTF7I7igajOwM8lTssnIg7aht6OwjzZDvYlFE7GNo/O1GfLzv/wyA7aioTO1u3BjvPo/Y6isbhOhatzjpAMb06DTCtOnSJnjooIJE6R9mEOoo4czpIpV46nM9LOtqROjpyySo611YcOh4dDzrPAQM6ZtnvOSqP2zlU/Mg5vPu3OVdrqDkCLJo5OSGNOekwgTk6hmw5/oNYORMzRjnQbjU5qBUmOeMIGDliLAs50Mz+OMw+6Tihg9U4tnPDOPPqsjhCyKM4We2VOHw+iTiWRHs4EANmONyNUjgMvkA4AnAwOAiDIThK2RM4cFcHOOTI9zfM0uI3pqLPNwwSvjfU/a0300WfN5vMkTckd4U3jll0N9qtXzfMwUw3AACAP2B9cD9R62E/QDtUP31fRz8qSzs/MfIvPzlJJT+YRRs/St0RP+UGCT+SuQA/B9rxPtky4z7wbtU+iIDIPrJavD5G8bA+2TimPrQmnD7CsJI+jc2JPjF0gT6oOHM+PHxkPl6kVj43o0k+xGs9PszxMT7VKSc+FgkdPm2FEz5VlQo+3y8CPkaZ9D18x+U9jNvXPYvHyj1ifr49xvOyPS4cqD3A7J09TFuUPT9eiz2c7II94vt1PZwUZz1+FFk9hu1LPY6SPz039zM95g8pPbTRHj1hMhU9TCgMPWyqAz2BYPc8oGPoPDVP2jwsFc08SqjAPCD8tDwABao89LefPK4Kljx+84w8TmmEPCTHeDyItGk8tItbPH8+TjyZv0E8ggI2PH77KjyCnyA8NOQWPNe/DTxFKQU80C/6O1oH6zv/ydw7gGnPO33YwjtiCrc7YPOrO2CIoTv2vpc7WI2OO1PqhTuGmns7F1xsOxcKXjs0llA7+PJDO78TODuq7Cw7kHIiO/WaGDsDXA87eKwGO0sH/TrCsu06/0vfOprE0ToMD8U6nh65Ol3nrToSXqM6M3iZOtkrkDq3b4c6IHZ+Ol0Lbzq5j2A6uPRSOr0sRjr/Kjo6fOMuOutKJDqyVho63fwQOhE0CDoJ5/857GXwOUnV4TmOJtQ5DEzHOeY4uzkI4a85AACAP8LudD8BWGo/cDZgPwCFVj/WPk0/UF9EP/7hOz+gwjM/KP0rP7KNJD+GcB0/FqIWP/oeED/w4wk/2+0DP4Bz/D6JifE+XBjnPr8a3T60i9M+dGbKPmymwT46R7k+sESxPsyaqT66RaI+zEGbPoOLlD6EH44+lvqHPqoZgj6Y83g+XDBuPj7kYz4VClo++JxQPiyYRz4w9z4+tbU2PpjPLj7nQCc+2QUgPtEaGT5bfBI+JScMPgUYBj72SwA+HYD1PRLj6j1+u+A9TATXPaS4zT3Z08Q9fFG8PU0ttD03Y6w9Wu+kPfTNnT16+5Y9gHSQPcM1ij0mPIQ9UQl9Pd8Ycj1+oWc9851dPUAJVD2R3ko9VxlCPSy1OT3hrTE9c/8pPQamIj3unRs9quMUPdpzDj1JSwg932YCPVSH+Ty1ve48eWvkPHuL2jzEGNE8ng7IPINovzwgIrc8VjevPCikpzzPZKA8rHWZPEfTkjxPeow8l2eGPBqYgDzPEXY8d25rPNxAYTzmg1c8tzJOPKVIRTw8wTw8OJg0PIPJLDw8USU8mysePBRVFzw6yhA8yYcKPJ6KBDx4n/07iKjyO/Eq6Dt1Id47FYfUO/ZWyzuHjMI7YiO6O1MXsjtTZKo7igajO0b6mzsDPJU7YsiOOyyciDtCtII7aht6O2RLbzsI82Q7AACAP6sfeD9gfXA/NRdpP1HrYT/p91o/QDtUP6izTT99X0c/Kz1BPypLOz/8hzU/MfIvP2OIKj85SSU/YjMgP5hFGz+gfhY/St0RP2tgDT/lBgk/oc8EP5K5AD9kh/k+B9rxPiJp6j7ZMuM+XjXcPvBu1T7g3c4+iIDIPlJVwj6yWrw+KY+2PkbxsD6ff6s+2TimPqIboT60Jpw+z1iXPsKwkj5hLY4+jc2JPi2QhT4xdIE+JvF6Pqg4cz76vGs+PHxkPp50XT5epFY+yQlQPjejST4Pb0M+xGs9PtSXNz7M8TE+QXgsPtUpJz40BSI+FgkdPjo0GD5thRM+gfsOPlWVCj7PUQY+3y8CPvVc/D1GmfQ9vhLtPXzH5T2ttd49jNvXPWQ30T2Lx8o9ZIrEPWJ+vj3/obg9xvOyPUxyrT0uHKg9GPCiPcDsnT3jEJk9TFuUPczKjz0/Xos9ixSHPZzsgj3Tyn094vt1PXFqbj2cFGc9jfhfPX4UWT21ZlI9hu1LPVSnRT2Okj89rK05PTf3Mz3AbS495g8pPVHcIz200R49zO4ZPWEyFT1DmxA9TCgMPWDYBz1sqgM9xDr/PIFg9zwXxO88oGPoPEI94Tw1T9o8vZfTPCwVzTzixcY8SqjAPN26ujwg/LQ8omqvPAAFqjzhyaQ89LefPPfNmjyuCpY8AACAP7pnej/C7nQ/aJRvPwFYaj/lOGU/cDZgPwJQWz8AhVY/zdRRP9Y+TT+Gwkg/UF9EP6YUQD/+4Ts/0sY3P6DCMz/n1C8/KP0rP+k6KD+yjSQ/DPUgP4ZwHT+u/xk/FqIWP1JXEz/6HhA/pfgMP/DjCT944AY/2+0DP70LAT+Ac/w+Fe/2PomJ8T4uQuw+XBjnPmwL4j6/Gt0+tkXYPrSL0z4m7M4+dGbKPhD6xT5spsE+/Gq9PjpHuT6iOrU+sESxPuhkrT7Mmqk+5OWlPrpFoj7XuZ4+zEGbPirdlz6Di5Q+b0yRPoQfjj5dBIs+lvqHPs8BhT6qGYI+joN+PpjzeD7CgnM+XDBuPr77aD4+5GM+O+lePhUKWj4xRlU++JxQPtENTD4smEc+ejtDPjD3Pj7Gyjo+tbU2Pnu3Mj6Yzy4+jv0qPudAJz4mmSM+2QUgPoyGHD7RGhk+OsIVPlt8Ej7LSA8+JScMPgMXCT4FGAY+yikDPvZLAD5V/Po9HYD1PZUi8D0S4+o97MDlPX674D0o0ts9TATXPVFR0j2kuM09qznJPdnTxD2ihsA9fFG8PeEzuD1NLbQ9Pj2wPTdjrD29nqg9Wu+kPZJUoT30zZ09EVuaPXr7lj3CrpM9gHSQPU1MjT3DNYo9gTCHPSY8hD1SWIE9UQl9PaCBdz3fGHI9AACAP/UHfD+rH3g/40Z0P2B9cD/kwmw/NRdpPxd6ZT9R62E/qmpeP+n3Wj/Yklc/QDtUP+zwUD+os00/PoNKP31fRz8ySEQ/Kz1BPzg+Pj8qSzs/0GM4P/yHNT+AtzI/MfIvP+A3LT9jiCo/j+MnPzlJJT83uSI/YjMgP4+3HT+YRRs/Vd0YP6B+Fj9UKRQ/St0RP12aDz9rYA0/Ti8LP+UGCT8M5wY/oc8EP4PAAj+SuQA/WHX9PmSH+T4IqfU+B9rxPiQa7j4iaeo+yMbmPtky4z4erd8+XjXcPmDL2D7wbtU+1x/SPuDdzj7XqMs+iIDIPsJkxT5SVcI+B1K/PrJavD4ib7k+KY+2Ppq6sz5G8bA+ADOuPp9/qz711qg+2TimPiGloz6iG6E+NpyePrQmnD70upk+z1iXPiAAlT7CsJI+jmqQPmEtjj4X+Ys+jc2JPp+qhz4tkIU+E36DPjF0gT7N5H4+JvF6Pi8Ndz6oOHM+VXNvPvq8az5aFWg+PHxkPmXxYD6edF0+rgVaPl6kVj55UFM+yQlQPhnQTD43o0k+74JGPg9vQz5mZ0A+xGs9Pvh7Oj7Ulzc+Kr80PszxMT6NLy8+QXgsPrzLKT7VKSc+YJIkPjQFIj4pgh8+FgkdPtOZGj46NBg+JdgVPm2FEz7tOxE+gfsOPgXEDD5VlQo+AACAP+gvfT+6Z3o/YKd3P8LudD/MPXI/aJRvP4DybD8BWGo/1MRnP+U4ZT8gtGI/cDZgP8K/XT8CUFs/HudYPwCFVj+WKVQ/zdRRP5OGTz/WPk0/gv1KP4bCSD/RjUY/UF9EP/I2Qj+mFEA/Wvg9P/7hOz+A0Tk/0sY3P+LBNT+gwjM//MgxP+fULz9Q5i0/KP0rP2AZKj/pOig/tGEmP7KNJD/UviI/DPUgP0wwHz+GcB0/q7UbP67/GT+BThg/FqIWP2D6FD9SVxM/37gRP/oeED+ViQ4/pfgMPx1sCz/w4wk/EmAIP3jgBj8UZQU/2+0DP8J6Aj+9CwE/gEH/PoBz/D5krfk+Fe/2Pn849D6JifE+IOLuPi5C7D6eqek+XBjnPlKO5D5sC+I+l4/fPr8a3T7QrNo+tkXYPl3l1T60i9M+qDjRPibszj4apsw+dGbKPiItyD4Q+sU+L83DPmymwT62hb8+/Gq9Pi5Wuz46R7k+ED63PqI6tT7cPLM+sESxPg9Srz7oZK0+LH2rPsyaqT65vac+5OWlPj4TpD66RaI+Rn2gPte5nj5e+5w+zEGbPhWNmT4q3Zc+/jGWPoOLlD6t6ZI+b0yRPruzjz6EH44+vo+MPl0Eiz5UfYk+lvqHPhl8hj7PAYU+rouDPqoZgj62q4A+joN+PqS3ez6Y83g+AACAP/8Bfj/1B3w/3BF6P6sfeD9bMXY/40Z0Pz1gcj9gfXA/RZ5uP+TCbD8362o/NRdpP9hGZz8XemU/7bBjP1HrYT89KWA/qmpeP5CvXD/p91o/rkNZP9iSVz9h5VU/QDtUP3KUUj/s8FA/rFBPP6izTT/bGUw/PoNKP8zvSD99X0c/TNJFPzJIRD8pwUI/Kz1BPzK8Pz84Pj4/OMM8PypLOz8I1jk/0GM4P3j0Nj/8hzU/Vh40P4C3Mj92UzE/MfIvP6uTLj/gNy0/yt4rP2OIKj+nNCk/j+MnPxeVJj85SSU/8P8jPze5Ij8KdSE/YjMgPzr0Hj+Ptx0/Wn0cP5hFGz9CEBo/Vd0YP8ysFz+gfhY/0FIVP1QpFD8pAhM/St0RP7K6ED9dmg8/R3wOP2tgDT/ERgw/Ti8LPwUaCj/lBgk/6PUHPwznBj9L2gU/oc8EPwvHAz+DwAI/B7wBP5K5AD8/cv8+WHX9Pmd8+z5kh/k+R5b3Pgip9T6gv/M+B9rxPjb47z4kGu4+yz/sPiJp6j4klug+yMbmPgb75D7ZMuM+OG7hPh6t3z6C790+XjXcPqp+2j5gy9g+ehvXPvBu1T68xdM+1x/SPjt90D7g3c4+wUHNPteoyz4cE8o+iIDIPhfxxj7CZMU+gtvDPlJVwj4r0sA+B1K/PuHUvT6yWrw+AACAP/aWfj/oL30/1sp7P7pnej+UBnk/YKd3PxtKdj/C7nQ/VJVzP8w9cj8p6HA/aJRvP4ZCbj+A8mw/VaRrPwFYaj+BDWk/1MRnP/Z9Zj/lOGU/nvVjPyC0Yj9mdGE/cDZgPzr6Xj/Cv10/BodcPwJQWz+2Glo/HudYPze1Vz8AhVY/dVZVP5YpVD9e/lI/zdRRP+CsUD+Thk8/5mFOP9Y+TT9gHUw/gv1KPzrfST+Gwkg/ZKdHP9GNRj/LdUU/UF9EP15KQz/yNkI/CiVBP6YUQD/ABT8/Wvg9P27sPD/+4Ts/BNk6P4DROT9wyzg/0sY3P6TDNj/iwTU/jME0P6DCMz8cxTI//MgxP0HOMD/n1C8/7NwuP1DmLT8P8Sw/KP0rP5kKKz9gGSo/eykpP+k6KD+nTSc/tGEmPw53JT+yjSQ/n6UjP9S+Ij9O2SE/DPUgPwwSID9MMB8/y08eP4ZwHT98khw/q7UbPxHaGj+u/xk/fiYZP4FOGD+0dxc/FqIWP6XNFT9g+hQ/RSgUP1JXEz+GhxI/37gRP1vrED/6HhA/uFMPP5WJDj+QwA0/pfgMP9UxDD8dbAs/fKcKP/DjCT94IQk/EmAIP72fBz944AY/PyIGPxRlBT/zqAQ/2+0DP8szAz/CegI/vcIBP70LAT++VQA/gEH/PoLZ/T6Ac/w+AACAP4AAfz//AX4/fAR9P/UHfD9rDHs/3BF6P0cYeT+rH3g/CCh3P1sxdj+kO3U/40Z0PxZTcz89YHI/Vm5xP2B9cD9bjW8/RZ5uPx6wbT/kwmw/mNZrPzfraj/BAGo/NRdpP5IuaD/YRmc/BGBmPxd6ZT8QlWQ/7bBjP67NYj9R62E/1wlhPz0pYD+ESV8/qmpeP66MXT+Qr1w/T9NbP+n3Wj9fHVo/rkNZP9dqWD/Yklc/sbtWP2HlVT/mD1U/QDtUP29nUz9ylFI/RsJRP+zwUD9kIFA/rFBPP8KBTj+os00/W+ZMP9sZTD8nTks/PoNKPyC5ST/M70g/QCdIP31fRz+BmEY/TNJFP9wMRT8ySEQ/TIRDPynBQj/J/kE/Kz1BP058QD8yvD8/1vw+Pzg+Pj9ZgD0/OMM8P9IGPD8qSzs/PJA6PwjWOT+QHDk/0GM4P8irNz949DY/3j02P/yHNT/O0jQ/Vh40P5FqMz+AtzI/IgUyP3ZTMT97ojA/MfIvP5ZCLz+rky4/b+UtP+A3LT//iiw/yt4rP0EzKz9jiCo/MN4pP6c0KT/Giyg/j+MnP/87Jz8XlSY/1e4lPzlJJT9CpCQ/8P8jP0JcIz83uSI/0BYiPwp1IT/l0yA/YjMgP36THz869B4/llUeP4+3HT8mGh0/Wn0cPyvhGz+YRRs/"


def _tab():
    return np.frombuffer(base64.b64decode(_EXPTAB_B64), dtype=np.float32).reshape(16, 129)


def host_consts():
    tab = _tab()
    t = np.arange(S, dtype=np.float64)
    inv = 1.0 / (10000.0 ** (np.arange(0, HD, 2, dtype=np.float64) / HD))
    emb = np.concatenate([np.outer(t, inv)] * 2, axis=-1)
    cos32 = np.cos(emb).astype(np.float32)
    sin32 = np.sin(emb).astype(np.float32)
    p = np.arange(128)
    pm = p % 64
    c_cos = np.ascontiguousarray(cos32[:, pm].T)
    sgn = np.where(pm < 32, -1.0, 1.0).astype(np.float32)
    c_sin = np.ascontiguousarray((sin32[:, pm] * sgn[None, :]).T)
    tt = np.arange(CH)
    c_mask = np.zeros((128, NH * CH), dtype=np.float32)
    for h in range(NH):
        dd = tt[None, :] - tt[:, None]
        c_mask[:, h * CH:(h + 1) * CH] = np.where(dd >= 0, tab[h][np.maximum(dd, 0)], np.float32(0))
    c_qdec = np.zeros((64, NH * CH), dtype=np.float32)
    for h in range(NH):
        c_qdec[:, h * CH:(h + 1) * CH] = tab[h][1:129][None, :]
    c_kdecc = np.zeros((128, NH), dtype=np.float32)
    for h in range(NH):
        c_kdecc[:, h] = tab[h][127 - tt]
    c_prot = np.zeros((128, 128), dtype=np.float32)
    c_prot[p, p ^ 32] = 1.0
    return {
        "c_cos": c_cos, "c_sin": c_sin, "c_mask": c_mask, "c_qdec": c_qdec,
        "c_kdecc": c_kdecc,
        "c_id32": np.eye(128, dtype=np.float32),
        "c_prot": c_prot,
        "c_ones1": np.ones((1, 128), dtype=np.float32),
        "c_onesc": np.ones((128, 1), dtype=np.float32),
        "c_eps": np.array([[EPS_LN, EPS_RMS]], dtype=np.float32),
    }


class Sched:
    ENGS = ("sync", "tensor", "vector", "scalar", "gpsimd")

    def __init__(self, nc, es):
        self.nc = nc
        self.ops = {e: [] for e in self.ENGS}
        self.sem = {k: es.enter_context(nc.semaphore("sem" + k)) for k in "LGTVCP"}
        self.cnt = {k: 0 for k in "LGTVCP"}

    def emit(self, eng, fn):
        self.ops[eng].append(fn)

    def wait(self, eng, s_, val):
        if val <= 0:
            return
        sem = self.sem[s_]
        self.emit(eng, lambda E, sem=sem, val=val: E.wait_ge(sem, val))

    def dma(self, eng, out, in_, s_, transpose=False, slow=False):
        self.cnt[s_] += 16
        sem = self.sem[s_]
        if transpose:
            self.emit(eng, lambda E, out=out, in_=in_, sem=sem:
                      E.dma_start_transpose(out=out, in_=in_).then_inc(sem, 16))
        elif slow:
            self.emit(eng, lambda E, out=out, in_=in_, sem=sem:
                      E.dma_start(out=out, in_=in_, allow_slow_non_contiguous=True).then_inc(sem, 16))
        else:
            self.emit(eng, lambda E, out=out, in_=in_, sem=sem:
                      E.dma_start(out=out, in_=in_).then_inc(sem, 16))
        return self.cnt[s_]

    def done(self, eng, s_):
        self.cnt[s_] += 1
        sem = self.sem[s_]

        def _f(E, sem=sem):
            E.drain()
            E.nop().then_inc(sem, 1)
        self.emit(eng, _f)
        return self.cnt[s_]


def build_program(phases=("0", "A", "B", "C", "D"), taps=()):
    nc = bass.Bass("TRN2", target_bir_lowering=False)
    T = {}

    def din(name, shape, dt=F32):
        T[name] = nc.dram_tensor(name, shape, dt, kind="ExternalInput").ap()
        return T[name]

    din("x", [S, D])
    din("wq", [D, D]); din("wk", [256, D]); din("wv", [256, D]); din("wo", [D, D])
    din("dwk", [D, 3]); din("dwb", [D]); din("pww", [D, D]); din("pwb", [D])
    din("rww", [8, D]); din("rwb", [8])
    din("expw", [8 * D, D]); din("expb", [8, D])
    din("outw", [D, D]); din("outb", [D]); din("normw", [D])
    din("ffdw", [256, D]); din("ffdb", [256]); din("ffuw", [D, 256]); din("ffub", [D])
    for n, shp, dt in [("c_cos", [128, S], F32), ("c_sin", [128, S], F32),
                       ("c_mask", [128, 2048], F32), ("c_qdec", [64, 2048], F32),
                       ("c_kdecc", [128, 16], F32),
                       ("c_id32", [128, 128], F32),
                       ("c_prot", [128, 128], F32), ("c_ones1", [1, 128], F32),
                       ("c_onesc", [128, 1], F32), ("c_eps", [1, 2], F32)]:
        din(n, shp, dt)
    T["out"] = nc.dram_tensor("out", [S, D], F32, kind="ExternalOutput").ap()

    BOUNCE = [("x", [S, D]), ("wq", [D, D]), ("wk", [256, D]), ("wv", [256, D]),
              ("wo", [D, D]), ("pww", [D, D]), ("expw", [8 * D, D]),
              ("outw", [D, D]), ("ffdw", [256, D]), ("ffuw", [D, 256]),
              ("rww", [8, D])]
    for n, shp in BOUNCE:
        T[n + "_hi"] = nc.dram_tensor(n + "_hi", shp, BF16, kind="Internal").ap()
        T[n + "_lo"] = nc.dram_tensor(n + "_lo", shp, BF16, kind="Internal").ap()
    T["xpost"] = nc.dram_tensor("xpost", [128, 16384], F32, kind="Internal").ap()
    T["rw_dram"] = nc.dram_tensor("rw_dram", [8, 2048], F32, kind="Internal").ap()

    tap_dram = {}
    for (tn, _fn, shp, dt) in taps:
        tap_dram[tn] = nc.dram_tensor("tap_" + tn, shp, dt, kind="ExternalOutput").ap()

    es = contextlib.ExitStack()
    with es:
        def sbt(name, shape, dt):
            T[name] = es.enter_context(nc.sbuf_tensor(name, shape, dt))
            return T[name]

        B16 = sbt("B16", [128, 65536], BF16)
        FA = sbt("FA", [128, 11264], F32)
        KVS = sbt("KVS", [64, 1024], F32)
        VTM = sbt("VTM", [128, 512], F32)
        KDK = sbt("KDK", [128, 1024], F32)
        QDEC = sbt("QDEC", [64, 2048], F32)
        CS = sbt("CS", [128, 1024], F32)
        ROWS = sbt("ROWS", [1, 512], F32)
        SML = sbt("SML", [128, 272], F32)
        RT = sbt("RT", [128, 1536], F32)
        LGS = sbt("LGS", [128, 64], F32)
        RTW = sbt("RTW", [128, 128], BF16)
        GHL = sbt("GHL", [128, 1024], BF16)
        ID32 = sbt("ID32", [128, 128], F32)
        PROT = sbt("PROT", [128, 128], F32)
        ONES1 = sbt("ONES1", [1, 128], F32)
        ONESC = sbt("ONESC", [128, 1], F32)
        EPSC = sbt("EPSC", [1, 2], F32)
        PS = [es.enter_context(nc.psum_tensor("ps%d" % i, [128, 512], F32)) for i in range(8)]

        s = Sched(nc, es)

        def v(fn):
            s.emit("vector", fn)

        def sc(fn):
            s.emit("scalar", fn)

        def pe(fn):
            s.emit("tensor", fn)

        def gp(fn):
            s.emit("gpsimd", fn)

        WQH, WQL = 0, 8192
        WKH, WKL = 16384, 18432
        WVH, WVL = 20480, 22528
        WOH, WOL = 24576, 32768
        XFP = [40960, 45056]
        OFH, OFL = 49152, 51200
        PWH, PWL = 0, 8192
        X1H, X1L = 16384, 32768
        XCH, XCL = 49152, 53248
        RWST = 61440
        EWP = [0, 8192]
        OWH, OWL = 0, 8192
        FDH, FDL = 49152, 51200
        FUH, FUL = 53248, 55296
        CBH, CBL = 57344, 59392
        HHI, HLO = 61440, 63488
        FSTS = 0
        FQ64 = 2048
        FK64 = 6144
        FQDQ = 7168
        FMASK = 9216
        FXB = [0, 4128]
        FRW8 = 9216
        FCMB = 0
        FSCR = 8192
        FXAD = 8192
        FXPJ = 10240

        def mm3(psl, wH, wL, xH, xL, kcs):
            n = len(kcs)
            for i, kc in enumerate(kcs):
                a, b_, c_, d_ = wH(kc), xH(kc), wL(kc), xL(kc)
                pe(lambda E, psl=psl, a=a, b=b_, st=(i == 0): E.matmul(psl, a, b, start=st, stop=False))
                pe(lambda E, psl=psl, a=a, d=d_: E.matmul(psl, a, d, start=False, stop=False))
                pe(lambda E, psl=psl, c=c_, b=b_, sp=(i == n - 1): E.matmul(psl, c, b, start=False, stop=sp))

        bias_col = {}
        consts_done = 0
        store_par = [0, 0]
        slab_done = {}
        state = {"spill": 0, "x1_done": 0}

        def emit_slab(nm, r0, r, C):
            par = emit_slab.par
            emit_slab.par ^= 1
            src = T[nm][r0:r0 + r, 0:C]
            wt = FA[0:r, par * 1024:par * 1024 + C]
            hi = B16[0:r, 57344 + par * 2048:57344 + par * 2048 + C]
            lo = B16[0:r, 57344 + par * 2048 + 1024:57344 + par * 2048 + 1024 + C]
            s.wait("sync", "G", store_par[par])
            lv = s.dma("sync", wt, src, "L")
            s.wait("vector", "L", lv)
            v(lambda E, hi=hi, wt=wt: E.tensor_copy(hi, wt))
            v(lambda E: E.drain())
            v(lambda E, lo=lo, wt=wt, hi=hi: E.tensor_tensor(lo, wt, hi, ALU.subtract))
            dv = s.done("vector", "V")
            s.wait("gpsimd", "V", dv)
            s.dma("gpsimd", T[nm + "_hi"][r0:r0 + r, 0:C], hi, "G")
            g = s.dma("gpsimd", T[nm + "_lo"][r0:r0 + r, 0:C], lo, "G")
            store_par[par] = g
            slab_done[nm] = g
        emit_slab.par = 0

        if "0" in phases:
            for cn, dst in [("c_id32", ID32[:, :]),
                            ("c_prot", PROT[:, :]), ("c_ones1", ONES1[:, :]),
                            ("c_onesc", ONESC[:, :]), ("c_kdecc", SML[:, 248:264]),
                            ("c_eps", EPSC[:, :]),
                            ("c_qdec", QDEC[:, :]),
                            ("c_mask", FA[:, FMASK:FMASK + 2048])]:
                s.dma("sync", dst, T[cn], "L")
            col = 0
            for bn, wid, ap in [
                ("dwb", 8, T["dwb"].rearrange("(c p) -> p c", p=128)),
                ("pwb", 8, T["pwb"].rearrange("(c p) -> p c", p=128)),
                ("outb", 8, T["outb"].rearrange("(c p) -> p c", p=128)),
                ("normw", 8, T["normw"].rearrange("(c p) -> p c", p=128)),
                ("ffub", 8, T["ffub"].rearrange("(c p) -> p c", p=128)),
                ("ffdb", 2, T["ffdb"].rearrange("(c p) -> p c", p=128)),
                ("dwk", 24, T["dwk"].rearrange("(c p) k -> p k c", p=128)),
                ("expb", 64, T["expb"].rearrange("e (c p) -> p e c", p=128)),
            ]:
                bias_col[bn] = col
                dst = SML[:, col:col + wid]
                if bn == "dwk":
                    dst = dst.rearrange("p (k c) -> p k c", c=8)
                elif bn == "expb":
                    dst = dst.rearrange("p (e c) -> p e c", c=8)
                s.dma("sync", dst, ap, "L", slow=True)
                col += wid
            bias_col["rwb"] = 240
            s.dma("sync", SML[0:8, 240:241], T["rwb"].unsqueeze(1), "L", slow=True)
            consts_done = s.cnt["L"]

            slab_jobs = []
            for n, shp in BOUNCE:
                R, C = shp
                for r0 in range(0, R, 128):
                    slab_jobs.append((n, r0, min(128, R - r0), C))
            prio = {"x": 0, "wq": 1, "wk": 2, "wv": 3, "wo": 4}
            slab_jobs.sort(key=lambda j: prio.get(j[0], 5))
            for j in slab_jobs:
                emit_slab(*j)

        def tload(dst_h, dst_l, nm, Rw, Cw, after_g, after_t=0, after_v=0):
            s.wait("sync", "G", after_g)
            s.wait("sync", "T", after_t)
            s.wait("sync", "V", after_v)
            last = 0
            for kc in range(Cw // 128):
                o = kc * Rw
                last = s.dma("sync", B16[:, dst_h + o:dst_h + o + Rw],
                             T[nm + "_hi"][0:Rw, kc * 128:(kc + 1) * 128], "L", transpose=True)
                last = s.dma("sync", B16[:, dst_l + o:dst_l + o + Rw],
                             T[nm + "_lo"][0:Rw, kc * 128:(kc + 1) * 128], "L", transpose=True)
            return last

        def emit_ln(xab, sqbuf, sqoff, epscol):
            vd = s.done("vector", "V")
            s.wait("scalar", "V", vd)
            for c in range(8):
                sc(lambda E, c=c, xab=xab, sqbuf=sqbuf, sqoff=sqoff: E.activation(
                    sqbuf[:, sqoff:sqoff + GT], FA[:, xab + c * GT:xab + (c + 1) * GT], ACT.Square))
                cd = s.done("scalar", "C")
                s.wait("tensor", "C", cd)
                pe(lambda E, c=c, xab=xab: E.matmul(PS[4][0:1, 0:GT], ONESC[:, :],
                                                    FA[:, xab + c * GT:xab + (c + 1) * GT],
                                                    start=(c == 0), stop=(c == 7)))
                pe(lambda E, c=c, sqbuf=sqbuf, sqoff=sqoff: E.matmul(
                    PS[5][0:1, 0:GT], ONESC[:, :], sqbuf[:, sqoff:sqoff + GT],
                    start=(c == 0), stop=(c == 7)))
                td = s.done("tensor", "T")
                s.wait("scalar", "T", td)
            td = s.done("tensor", "T")
            s.wait("vector", "T", td)
            v(lambda E: E.tensor_copy(ROWS[0:1, 0:GT], PS[4][0:1, 0:GT]))
            v(lambda E: E.tensor_copy(ROWS[0:1, 256:256 + GT], PS[5][0:1, 0:GT]))
            v(lambda E: E.drain())
            v(lambda E: E.tensor_scalar(ROWS[0:1, 0:GT], ROWS[0:1, 0:GT], 1.0 / D, None, ALU.mult))
            v(lambda E: E.tensor_scalar(ROWS[0:1, 256:256 + GT], ROWS[0:1, 256:256 + GT], 1.0 / D, None, ALU.mult))
            v(lambda E: E.drain())
            v(lambda E: E.tensor_tensor(VTM[0:1, 256:256 + GT], ROWS[0:1, 0:GT], ROWS[0:1, 0:GT], ALU.mult))
            v(lambda E: E.drain())
            v(lambda E: E.tensor_tensor(ROWS[0:1, 256:256 + GT], ROWS[0:1, 256:256 + GT],
                                        VTM[0:1, 256:256 + GT], ALU.subtract))
            vd = s.done("vector", "V")
            s.wait("scalar", "V", vd)
            sc(lambda E, epscol=epscol: E.activation(ROWS[0:1, 256:256 + GT], ROWS[0:1, 256:256 + GT],
                                               ACT.Sqrt, bias=EPSC[0:1, epscol:epscol + 1]))
            cd = s.done("scalar", "C")
            s.wait("vector", "C", cd)
            v(lambda E: E.reciprocal(VTM[0:1, 0:GT], ROWS[0:1, 256:256 + GT]))
            vd = s.done("vector", "V")
            s.wait("tensor", "V", vd)
            pe(lambda E: E.matmul(PS[4][:, 0:GT], ONES1[:, :], ROWS[0:1, 0:GT], start=True, stop=True))
            pe(lambda E: E.matmul(PS[5][:, 0:GT], ONES1[:, :], VTM[0:1, 0:GT], start=True, stop=True))
            td = s.done("tensor", "T")
            s.wait("vector", "T", td)
            for c in range(8):
                v(lambda E, c=c, xab=xab: E.tensor_tensor(FA[:, xab + c * GT:xab + (c + 1) * GT],
                                                          FA[:, xab + c * GT:xab + (c + 1) * GT],
                                                          PS[4][:, 0:GT], ALU.subtract))
            v(lambda E: E.drain())
            for c in range(8):
                v(lambda E, c=c, xab=xab: E.tensor_tensor(FA[:, xab + c * GT:xab + (c + 1) * GT],
                                                          FA[:, xab + c * GT:xab + (c + 1) * GT],
                                                          PS[5][:, 0:GT], ALU.mult))
            vd = s.done("vector", "V")
            s.wait("tensor", "V", vd)
            return vd

        group_t_end = [0] * (NG + 2)
        xf_ready = [0, 0]
        cs_ready = [0, 0]

        def prefetch_group(g):
            par = g % 2
            base = XFP[par]
            t0 = g * GT
            s.wait("sync", "G", slab_done.get("x", 0))
            s.wait("sync", "T", group_t_end[g - 2] if g >= 2 else 0)
            lv = 0
            for kc in range(8):
                lv = s.dma("sync", B16[:, base + kc * 256:base + (kc + 1) * 256],
                           T["x_hi"][t0:t0 + GT, kc * 128:(kc + 1) * 128], "L", transpose=True)
                lv = s.dma("sync", B16[:, base + 2048 + kc * 256:base + 2048 + (kc + 1) * 256],
                           T["x_lo"][t0:t0 + GT, kc * 128:(kc + 1) * 128], "L", transpose=True)
            xf_ready[par] = lv
            lv = s.dma("sync", CS[:, par * 512:par * 512 + 256], T["c_cos"][:, t0:t0 + GT], "L")
            lv = s.dma("sync", CS[:, par * 512 + 256:par * 512 + 512], T["c_sin"][:, t0:t0 + GT], "L")
            cs_ready[par] = lv

        def emit_groupA(g, wA):
            par = g % 2
            XH = XFP[par]
            XL = XFP[par] + 2048
            t0 = g * GT
            cs_cos = CS[:, par * 512:par * 512 + 256]
            cs_sin = CS[:, par * 512 + 256:par * 512 + 512]
            if g + 1 < NG:
                prefetch_group(g + 1)
            s.wait("tensor", "L", max(xf_ready[par], wA, consts_done))
            s.wait("vector", "L", max(xf_ready[par], cs_ready[par]))
            s.wait("vector", "G", state["spill"])

            def rope_crow(c, bank, dst64):
                qb = (c % 2) * 256
                tsin = 512 + (c % 2) * 256
                tcos = 1024
                psq = PS[bank][:, 0:GT]
                ps2 = PS[2][:, (c % 2) * 256:(c % 2) * 256 + GT]
                td = s.done("tensor", "T")
                s.wait("vector", "T", td)
                v(lambda E, qb=qb, psq=psq: E.tensor_copy(RT[:, qb:qb + GT], psq))
                vd = s.done("vector", "V")
                s.wait("tensor", "V", vd)
                pe(lambda E, ps2=ps2, qb=qb: E.matmul(ps2, PROT[:, :], RT[:, qb:qb + GT], start=True, stop=True))
                t2 = s.done("tensor", "T")
                s.wait("vector", "T", t2)
                v(lambda E, tsin=tsin, ps2=ps2, cs_sin=cs_sin: E.tensor_tensor(
                    RT[:, tsin:tsin + GT], ps2, cs_sin, ALU.mult))
                v2 = s.done("vector", "V")
                s.wait("gpsimd", "V", v2)
                gp(lambda E, tcos=tcos, qb=qb, cs_cos=cs_cos: E.tensor_tensor(
                    RT[:, tcos:tcos + GT], RT[:, qb:qb + GT], cs_cos, ALU.mult))
                gp(lambda E: E.drain())
                gp(lambda E, dst64=dst64, tcos=tcos, tsin=tsin: E.tensor_tensor(
                    FA[0:64, dst64:dst64 + GT], RT[0:64, tcos:tcos + GT],
                    RT[0:64, tsin:tsin + GT], ALU.add))
                gp(lambda E, dst64=dst64, tcos=tcos, tsin=tsin: E.tensor_tensor(
                    FA[0:64, dst64 + GT:dst64 + 2 * GT], RT[64:128, tcos:tcos + GT],
                    RT[64:128, tsin:tsin + GT], ALU.add))
                return s.done("gpsimd", "P")

            rope_p = 0
            for c in range(8):
                bank = c % 2
                mm3(PS[bank][:, 0:GT],
                    lambda kc, c=c: B16[:, WQH + kc * 1024 + c * 128:WQH + kc * 1024 + (c + 1) * 128],
                    lambda kc, c=c: B16[:, WQL + kc * 1024 + c * 128:WQL + kc * 1024 + (c + 1) * 128],
                    lambda kc, XH=XH: B16[:, XH + kc * 256:XH + (kc + 1) * 256],
                    lambda kc, XL=XL: B16[:, XL + kc * 256:XL + (kc + 1) * 256],
                    list(range(8)))
                rope_p = rope_crow(c, bank, FQ64 + (2 * c) * GT)
            for c in range(2):
                bank = c % 2
                mm3(PS[bank][:, 0:GT],
                    lambda kc, c=c: B16[:, WKH + kc * 256 + c * 128:WKH + kc * 256 + (c + 1) * 128],
                    lambda kc, c=c: B16[:, WKL + kc * 256 + c * 128:WKL + kc * 256 + (c + 1) * 128],
                    lambda kc, XH=XH: B16[:, XH + kc * 256:XH + (kc + 1) * 256],
                    lambda kc, XL=XL: B16[:, XL + kc * 256:XL + (kc + 1) * 256],
                    list(range(8)))
                rope_p = rope_crow(c, bank, FK64 + (2 * c) * GT)

            for chx in range(2):
                mm3(PS[3][:, 0:256],
                    lambda kc, XH=XH, chx=chx: B16[:, XH + kc * 256 + chx * 128:XH + kc * 256 + chx * 128 + 128],
                    lambda kc, XL=XL, chx=chx: B16[:, XL + kc * 256 + chx * 128:XL + kc * 256 + chx * 128 + 128],
                    lambda kc: B16[:, WVH + kc * 256:WVH + (kc + 1) * 256],
                    lambda kc: B16[:, WVL + kc * 256:WVL + (kc + 1) * 256],
                    list(range(8)))
                td = s.done("tensor", "T")
                s.wait("vector", "T", td)
                v(lambda E, chx=chx: E.tensor_copy(VTM[:, chx * 256:chx * 256 + 256], PS[3][:, 0:256]))
                vd = s.done("vector", "V")
                s.wait("tensor", "V", vd)

            s.wait("tensor", "P", rope_p)
            s.wait("vector", "P", rope_p)
            for chx in range(2):
                for gkv in range(4):
                    pe(lambda E, gkv=gkv, chx=chx: E.transpose(
                        PS[2][:, gkv * 64:(gkv + 1) * 64],
                        FA[0:64, FK64 + gkv * GT + chx * 128:FK64 + gkv * GT + chx * 128 + 128],
                        ID32[0:64, 0:64]))
                td = s.done("tensor", "T")
                s.wait("vector", "T", td)
                for hh in range(16):
                    v(lambda E, hh=hh: E.tensor_scalar(
                        KDK[:, hh * 64:(hh + 1) * 64], PS[2][:, (hh // 4) * 64:(hh // 4) * 64 + 64],
                        SML[:, 248 + hh:249 + hh], None, ALU.mult))
                v(lambda E: E.drain())
                v(lambda E, chx=chx: E.tensor_tensor(
                    FA[0:64, FQDQ:FQDQ + 2048].rearrange("p (h t) -> p h t", t=128),
                    FA[0:64, FQ64:FQ64 + 4096].rearrange("p (h t) -> p h t", t=GT)[:, :, chx * 128:chx * 128 + 128],
                    QDEC[:, :].rearrange("p (h t) -> p h t", t=128), ALU.mult))
                kq = s.done("vector", "V")
                s.wait("tensor", "V", kq)
                for r in range(2):
                    for j in range(8):
                        hh = r * 8 + j
                        gkv = hh // 4
                        bank = 4 + (j // 4)
                        pe(lambda E, hh=hh, gkv=gkv, bank=bank, j=j, chx=chx: E.matmul(
                            PS[bank][:, (j % 4) * 128:(j % 4) * 128 + 128],
                            FA[0:64, FK64 + gkv * GT + chx * 128:FK64 + gkv * GT + chx * 128 + 128],
                            FA[0:64, FQ64 + hh * GT + chx * 128:FQ64 + hh * GT + chx * 128 + 128],
                            start=True, stop=True))
                    td = s.done("tensor", "T")
                    s.wait("vector", "T", td)
                    for b in range(2):
                        v(lambda E, r=r, b=b: E.tensor_tensor(
                            FA[:, FSTS + r * 1024 + b * 512:FSTS + r * 1024 + (b + 1) * 512],
                            PS[4 + b][:, 0:512],
                            FA[:, FMASK + (r * 8 + b * 4) * 128:FMASK + (r * 8 + b * 4 + 4) * 128], ALU.mult))
                    std = s.done("vector", "V")
                    s.wait("tensor", "V", std)
                    for j in range(8):
                        hh = r * 8 + j
                        gkv = hh // 4
                        ob = 6 if j < 4 else 7
                        pe(lambda E, hh=hh, gkv=gkv, ob=ob, j=j, chx=chx: E.matmul(
                            PS[ob][0:64, (j % 4) * 128:(j % 4) * 128 + 128],
                            VTM[:, chx * 256 + gkv * 64:chx * 256 + gkv * 64 + 64],
                            FA[:, FSTS + hh * 128:FSTS + (hh + 1) * 128],
                            start=True, stop=False))
                        pe(lambda E, hh=hh, ob=ob, j=j: E.matmul(
                            PS[ob][0:64, (j % 4) * 128:(j % 4) * 128 + 128],
                            KVS[0:64, hh * 64:(hh + 1) * 64],
                            FA[0:64, FQDQ + hh * 128:FQDQ + (hh + 1) * 128],
                            start=False, stop=True))
                    for j in range(8):
                        hh = r * 8 + j
                        gkv = hh // 4
                        pe(lambda E, hh=hh, gkv=gkv, j=j, chx=chx: E.matmul(
                            PS[3][0:64, j * 64:(j + 1) * 64],
                            KDK[:, hh * 64:(hh + 1) * 64],
                            VTM[:, chx * 256 + gkv * 64:chx * 256 + gkv * 64 + 64],
                            start=True, stop=True))
                    td = s.done("tensor", "T")
                    s.wait("vector", "T", td)
                    for j in range(8):
                        hh = r * 8 + j
                        bd = float(_tab()[hh][128])
                        v(lambda E, hh=hh, j=j, bd=bd, r=r: E.scalar_tensor_tensor(
                            KVS[:, hh * 64:(hh + 1) * 64], KVS[:, hh * 64:(hh + 1) * 64],
                            bd, PS[3][0:64, j * 64:(j + 1) * 64], ALU.mult, ALU.add))
                    kvd = s.done("vector", "V")
                    s.wait("scalar", "T", td)
                    for j in range(8):
                        hh = r * 8 + j
                        cr = hh // 2
                        po = 64 * (hh % 2)
                        b = 6 if j < 4 else 7
                        sc(lambda E, b=b, j=j, cr=cr, po=po, chx=chx: E.activation(
                            B16[po:po + 64, OFH + cr * 256 + chx * 128:OFH + cr * 256 + chx * 128 + 128],
                            PS[b][0:64, (j % 4) * 128:(j % 4) * 128 + 128], ACT.Copy))
                    hid = s.done("scalar", "C")
                    s.wait("vector", "C", hid)
                    for j in range(8):
                        hh = r * 8 + j
                        cr = hh // 2
                        po = 64 * (hh % 2)
                        b = 6 if j < 4 else 7
                        v(lambda E, b=b, j=j, cr=cr, po=po, chx=chx: E.tensor_tensor(
                            B16[po:po + 64, OFL + cr * 256 + chx * 128:OFL + cr * 256 + chx * 128 + 128],
                            PS[b][0:64, (j % 4) * 128:(j % 4) * 128 + 128],
                            B16[po:po + 64, OFH + cr * 256 + chx * 128:OFH + cr * 256 + chx * 128 + 128],
                            ALU.subtract))
                    oev = s.done("vector", "V")
                    s.wait("tensor", "V", max(kvd, oev))

            for c in range(8):
                bank = c % 2
                mm3(PS[bank][:, 0:GT],
                    lambda kc, c=c: B16[:, WOH + kc * 1024 + c * 128:WOH + kc * 1024 + (c + 1) * 128],
                    lambda kc, c=c: B16[:, WOL + kc * 1024 + c * 128:WOL + kc * 1024 + (c + 1) * 128],
                    lambda kc: B16[:, OFH + kc * 256:OFH + (kc + 1) * 256],
                    lambda kc: B16[:, OFL + kc * 256:OFL + (kc + 1) * 256],
                    list(range(8)))
                td = s.done("tensor", "T")
                s.wait("vector", "T", td)
                v(lambda E, c=c, bank=bank, XH=XH: E.tensor_tensor(
                    FA[:, c * GT:(c + 1) * GT], PS[bank][:, 0:GT],
                    B16[:, XH + c * 256:XH + (c + 1) * 256], ALU.add))
                v(lambda E: E.drain())
                v(lambda E, c=c, XL=XL: E.tensor_tensor(
                    FA[:, c * GT:(c + 1) * GT], FA[:, c * GT:(c + 1) * GT],
                    B16[:, XL + c * 256:XL + (c + 1) * 256], ALU.add))
                vd = s.done("vector", "V")
                s.wait("tensor", "V", vd)
            vd = emit_ln(0, RT, 1280, 0)
            s.wait("gpsimd", "V", vd)
            state["spill"] = s.dma("gpsimd",
                                   T["xpost"].rearrange("p (c t) -> p c t", c=8)[:, :, t0:t0 + GT],
                                   FA[:, 0:2048].rearrange("p (c t) -> p c t", c=8), "G")
            group_t_end[g] = s.cnt["T"]

        if "A" in phases:
            wq_t = tload(WQH, WQL, "wq", 1024, 1024, slab_done.get("wq", 0))
            wk_t = tload(WKH, WKL, "wk", 256, 1024, slab_done.get("wk", 0))
            wv_t = tload(WVH, WVL, "wv", 256, 1024, slab_done.get("wv", 0))
            wo_t = tload(WOH, WOL, "wo", 1024, 1024, slab_done.get("wo", 0))
            wA = max(wq_t, wk_t, wv_t, wo_t)
            v(lambda E: E.memset(KVS[:, :], 0.0))
            kvz = s.done("vector", "V")
            s.wait("tensor", "V", kvz)
            prefetch_group(0)
            for g in range(NG):
                emit_groupA(g, wA)

        xb_ready = [0, 0]
        b0_used = [0, 0]
        b0_rw_used = [0]

        def b0_load(bg):
            par = bg % 2
            base = FXB[par]
            t0 = bg * BT
            xpv = T["xpost"].rearrange("p (c t) -> p c t", c=8)
            s.wait("sync", "G", state["spill"])
            s.wait("sync", "V", b0_used[par])
            dst = FA[:, base:base + 8 * 516].rearrange("p (c t) -> p c t", t=516)
            if bg == 0:
                lv = s.dma("sync", dst[:, :, 2:2 + BT], xpv[:, :, 0:BT], "L")
            else:
                lv = s.dma("sync", dst[:, :, 0:BT + 2], xpv[:, :, t0 - 2:t0 + BT], "L")
            xb_ready[par] = lv

        def emit_b0(bg, pw_t):
            par = bg % 2
            base = FXB[par]
            t0 = bg * BT
            if bg + 1 < S // BT:
                b0_load(bg + 1)
            s.wait("vector", "L", max(xb_ready[par], consts_done))
            if bg == 0:
                v(lambda E, base=base: E.memset(
                    FA[:, base:base + 8 * 516].rearrange("p (c t) -> p c t", t=516)[:, :, 0:2], 0.0))
                v(lambda E: E.drain())
            dk = bias_col["dwk"]
            db = bias_col["dwb"]
            for c in range(8):
                xb = FA[:, base + c * 516:base + (c + 1) * 516]
                v(lambda E, xb=xb, c=c, dk=dk, db=db: E.tensor_scalar(
                    KDK[:, 512:1024], xb[:, 0:BT], SML[:, dk + c:dk + c + 1],
                    SML[:, db + c:db + c + 1], ALU.mult, ALU.add))
                v(lambda E: E.drain())
                v(lambda E, xb=xb, c=c, dk=dk: E.scalar_tensor_tensor(
                    KDK[:, 512:1024], xb[:, 1:BT + 1], SML[:, dk + 8 + c:dk + 8 + c + 1],
                    KDK[:, 512:1024], ALU.mult, ALU.add))
                v(lambda E: E.drain())
                v(lambda E, xb=xb, c=c, dk=dk: E.scalar_tensor_tensor(
                    KDK[:, 512:1024], xb[:, 2:BT + 2], SML[:, dk + 16 + c:dk + 16 + c + 1],
                    KDK[:, 512:1024], ALU.mult, ALU.add))
                v(lambda E: E.drain())
                v(lambda E, c=c: E.tensor_copy(B16[:, XCH + c * 512:XCH + (c + 1) * 512], KDK[:, 512:1024]))
                v(lambda E: E.drain())
                v(lambda E, c=c: E.tensor_tensor(B16[:, XCL + c * 512:XCL + (c + 1) * 512],
                                                 KDK[:, 512:1024],
                                                 B16[:, XCH + c * 512:XCH + (c + 1) * 512], ALU.subtract))
                v(lambda E: E.drain())
            vd = s.done("vector", "V")
            s.wait("tensor", "V", vd)
            s.wait("tensor", "L", pw_t)
            pb = bias_col["pwb"]
            for c in range(8):
                bank = c % 2
                mm3(PS[bank][:, 0:BT],
                    lambda kc, c=c: B16[:, PWH + kc * 1024 + c * 128:PWH + kc * 1024 + (c + 1) * 128],
                    lambda kc, c=c: B16[:, PWL + kc * 1024 + c * 128:PWL + kc * 1024 + (c + 1) * 128],
                    lambda kc: B16[:, XCH + kc * 512:XCH + (kc + 1) * 512],
                    lambda kc: B16[:, XCL + kc * 512:XCL + (kc + 1) * 512],
                    list(range(8)))
                td = s.done("tensor", "T")
                s.wait("scalar", "T", td)
                sc(lambda E, bank=bank, c=c, pb=pb: E.activation(
                    KDK[:, 0:512], PS[bank][:, 0:BT], ACT.Silu, bias=SML[:, pb + c:pb + c + 1]))
                cd = s.done("scalar", "C")
                s.wait("vector", "C", cd)
                v(lambda E, base=base, c=c: E.tensor_tensor(
                    KDK[:, 0:512], FA[:, base + c * 516 + 2:base + c * 516 + 2 + BT],
                    KDK[:, 0:512], ALU.add))
                v(lambda E: E.drain())
                v(lambda E, c=c, t0=t0: E.tensor_copy(
                    B16[:, X1H + c * 2048 + t0:X1H + c * 2048 + t0 + BT], KDK[:, 0:512]))
                v(lambda E: E.drain())
                v(lambda E, c=c, t0=t0: E.tensor_tensor(
                    B16[:, X1L + c * 2048 + t0:X1L + c * 2048 + t0 + BT], KDK[:, 0:512],
                    B16[:, X1H + c * 2048 + t0:X1H + c * 2048 + t0 + BT], ALU.subtract))
                vd = s.done("vector", "V")
                s.wait("tensor", "V", vd)
            b0_used[par] = vd
            for kc in range(8):
                pe(lambda E, kc=kc, t0=t0: E.matmul(
                    PS[2][0:8, 0:BT], RTW[:, kc * 8:(kc + 1) * 8],
                    B16[:, X1H + kc * 2048 + t0:X1H + kc * 2048 + t0 + BT],
                    start=(kc == 0), stop=False))
                pe(lambda E, kc=kc, t0=t0: E.matmul(
                    PS[2][0:8, 0:BT], RTW[:, 64 + kc * 8:64 + (kc + 1) * 8],
                    B16[:, X1H + kc * 2048 + t0:X1H + kc * 2048 + t0 + BT],
                    start=False, stop=False))
                pe(lambda E, kc=kc, t0=t0: E.matmul(
                    PS[2][0:8, 0:BT], RTW[:, kc * 8:(kc + 1) * 8],
                    B16[:, X1L + kc * 2048 + t0:X1L + kc * 2048 + t0 + BT],
                    start=False, stop=(kc == 7)))
            td = s.done("tensor", "T")
            s.wait("vector", "T", td)
            rb = bias_col["rwb"]
            v(lambda E, rb=rb: E.tensor_scalar(KDK[0:8, 512:1024], PS[2][0:8, 0:BT],
                                               SML[0:8, rb:rb + 1], None, ALU.add))
            vd = s.done("vector", "V")
            s.wait("tensor", "V", vd)
            for sbx in range(4):
                pe(lambda E, sbx=sbx: E.transpose(
                    PS[2][:, 0:8], KDK[0:8, 512 + sbx * 128:512 + (sbx + 1) * 128], ID32[0:8, 0:8]))
                td = s.done("tensor", "T")
                s.wait("vector", "T", td)
                v(lambda E: E.tensor_copy(LGS[:, 0:8], PS[2][:, 0:8]))
                v(lambda E: E.drain())
                v(lambda E: E.max(LGS[:, 16:24], LGS[:, 0:8]))
                v(lambda E: E.drain())
                v(lambda E: E.tensor_single_scalar(LGS[:, 24:32], LGS[:, 0:8], LGS[:, 16:17], ALU.subtract))
                v(lambda E: E.tensor_single_scalar(LGS[:, 32:40], LGS[:, 0:8], LGS[:, 17:18], ALU.is_ge))
                vd = s.done("vector", "V")
                s.wait("scalar", "V", vd)
                sc(lambda E: E.activation(LGS[:, 40:48], LGS[:, 24:32], ACT.Exp))
                cd = s.done("scalar", "C")
                s.wait("vector", "C", cd)
                v(lambda E: E.memset(LGS[:, 56:64], 1.0))
                v(lambda E: E.tensor_tensor(LGS[:, 48:56], LGS[:, 40:48], LGS[:, 32:40], ALU.mult))
                v(lambda E: E.drain())
                v(lambda E: E.tensor_reduce(LGS[:, 56:57], LGS[:, 48:56], AXX, ALU.add))
                v(lambda E: E.drain())
                v(lambda E: E.reciprocal(LGS[:, 40:48], LGS[:, 56:64]))
                v(lambda E: E.drain())
                v(lambda E: E.tensor_single_scalar(LGS[:, 8:16], LGS[:, 48:56], LGS[:, 40:41], ALU.mult))
                vd = s.done("vector", "V")
                s.wait("tensor", "V", vd)
                pe(lambda E: E.transpose(PS[2][0:8, 128:256], LGS[:, 8:16], ID32[:, :]))
                td = s.done("tensor", "T")
                s.wait("vector", "T", td)
                if sbx == 0:
                    s.wait("vector", "G", b0_rw_used[0])
                v(lambda E, sbx=sbx: E.tensor_copy(
                    CS[0:8, sbx * 128:(sbx + 1) * 128], PS[2][0:8, 128:256]))
                vd = s.done("vector", "V")
                s.wait("tensor", "V", vd)
            s.wait("gpsimd", "V", vd)
            b0_rw_used[0] = s.dma("gpsimd", T["rw_dram"][0:8, t0:t0 + BT], CS[0:8, 0:512], "G")

        if "B" in phases:
            pw_t = tload(PWH, PWL, "pww", 1024, 1024, slab_done.get("pww", 0),
                         after_t=s.cnt["T"])
            s.wait("sync", "G", slab_done.get("rww", 0))
            rwv_h = T["rww_hi"].rearrange("e (kc p) -> p kc e", p=128)
            rwv_l = T["rww_lo"].rearrange("e (kc p) -> p kc e", p=128)
            rl = 0
            for kc in range(8):
                s.dma("sync", RTW[:, kc * 8:(kc + 1) * 8], rwv_h[:, kc, :], "L", slow=True)
                rl = s.dma("sync", RTW[:, 64 + kc * 8:64 + (kc + 1) * 8], rwv_l[:, kc, :], "L", slow=True)
            s.wait("tensor", "L", rl)
            b0_load(0)
            for bg in range(S // BT):
                emit_b0(bg, pw_t)
            state["x1_done"] = s.cnt["V"]

        ew_t = [0, 0]
        ew_free = [0, 0]
        crw_used = [0]

        def c_prefetch(e, oh, par, gate_t):
            s.wait("sync", "G", slab_done.get("expw", 0))
            s.wait("sync", "V", ew_free[par])
            s.wait("sync", "T", gate_t)
            base = EWP[par]
            lv = 0
            r0 = e * 1024 + oh * 512
            for kc in range(8):
                lv = s.dma("sync", B16[:, base + kc * 512:base + (kc + 1) * 512],
                           T["expw_hi"][r0:r0 + 512, kc * 128:(kc + 1) * 128], "L", transpose=True)
                lv = s.dma("sync", B16[:, base + 4096 + kc * 512:base + 4096 + (kc + 1) * 512],
                           T["expw_lo"][r0:r0 + 512, kc * 128:(kc + 1) * 128], "L", transpose=True)
            ew_t[par] = lv

        def emit_cexp(sg0, e, oh, par):
            eb = bias_col["expb"]
            base = EWP[par]
            s.wait("tensor", "L", ew_t[par])
            for th in range(2):
                tb = sg0 + th * 512
                s.wait("sync", "V", s.cnt["V"])
                s.wait("sync", "G", s.cnt["G"])
                lv = s.dma("sync", CS[0:8, 0:512], T["rw_dram"][0:8, tb:tb + 512], "L")
                s.wait("tensor", "L", lv)
                pe(lambda E: E.matmul(PS[2][0:1, 0:512], ID32[0:8, e:e + 1],
                                      CS[0:8, 0:512], start=True, stop=True))
                td = s.done("tensor", "T")
                s.wait("vector", "T", td)
                v(lambda E: E.tensor_copy(ROWS[0:1, 0:512], PS[2][0:1, 0:512]))
                vd = s.done("vector", "V")
                crw_used[0] = vd
                s.wait("tensor", "V", vd)
                pe(lambda E: E.matmul(PS[2][:, 0:512], ONES1[:, :], ROWS[0:1, 0:512], start=True, stop=True))
                td = s.done("tensor", "T")
                s.wait("vector", "T", td)
                for oc in range(4):
                    co = oh * 4 + oc
                    bank = oc % 2
                    mm3(PS[bank][:, 0:512],
                        lambda kc, base=base, oc=oc: B16[:, base + kc * 512 + oc * 128:base + kc * 512 + oc * 128 + 128],
                        lambda kc, base=base, oc=oc: B16[:, base + 4096 + kc * 512 + oc * 128:base + 4096 + kc * 512 + oc * 128 + 128],
                        lambda kc, tb=tb: B16[:, X1H + kc * 2048 + tb:X1H + kc * 2048 + tb + 512],
                        lambda kc, tb=tb: B16[:, X1L + kc * 2048 + tb:X1L + kc * 2048 + tb + 512],
                        list(range(8)))
                    td = s.done("tensor", "T")
                    s.wait("scalar", "T", td)
                    sc(lambda E, bank=bank, e=e, co=co, eb=eb: E.activation(
                        FA[:, FSCR:FSCR + 512], PS[bank][:, 0:512], ACT.Silu,
                        bias=SML[:, eb + e * 8 + co:eb + e * 8 + co + 1]))
                    cd = s.done("scalar", "C")
                    s.wait("vector", "C", cd)
                    dst_off = FCMB + co * 1024 + th * 512
                    if e == 0:
                        v(lambda E, dst_off=dst_off: E.tensor_tensor(
                            FA[:, dst_off:dst_off + 512], FA[:, FSCR:FSCR + 512], PS[2][:, 0:512], ALU.mult))
                    else:
                        v(lambda E: E.tensor_tensor(
                            FA[:, FSCR + 512:FSCR + 1024], FA[:, FSCR:FSCR + 512], PS[2][:, 0:512], ALU.mult))
                        v(lambda E: E.drain())
                        v(lambda E, dst_off=dst_off: E.tensor_tensor(
                            FA[:, dst_off:dst_off + 512], FA[:, dst_off:dst_off + 512],
                            FA[:, FSCR + 512:FSCR + 1024], ALU.add))
                    vd = s.done("vector", "V")
                    s.wait("tensor", "V", vd)
            ew_free[par] = s.cnt["V"]

        xp_used = [0, 0]

        def emit_dgroup(sg0, dg):
            t0 = sg0 + dg * GT
            toff = (sg0 % SGT) + dg * GT
            toff = dg * GT
            ob = bias_col["outb"]
            nw = bias_col["normw"]
            fb = bias_col["ffdb"]
            ub = bias_col["ffub"]
            for co in range(8):
                v(lambda E, co=co, toff=toff: E.tensor_copy(
                    B16[:, CBH + co * 256:CBH + (co + 1) * 256],
                    FA[:, FCMB + co * 1024 + toff:FCMB + co * 1024 + toff + GT]))
                v(lambda E: E.drain())
                v(lambda E, co=co, toff=toff: E.tensor_tensor(
                    B16[:, CBL + co * 256:CBL + (co + 1) * 256],
                    FA[:, FCMB + co * 1024 + toff:FCMB + co * 1024 + toff + GT],
                    B16[:, CBH + co * 256:CBH + (co + 1) * 256], ALU.subtract))
                v(lambda E: E.drain())
            vd = s.done("vector", "V")
            s.wait("tensor", "V", vd)
            for c in range(8):
                bank = c % 2
                mm3(PS[bank][:, 0:GT],
                    lambda kc, c=c: B16[:, OWH + kc * 1024 + c * 128:OWH + kc * 1024 + (c + 1) * 128],
                    lambda kc, c=c: B16[:, OWL + kc * 1024 + c * 128:OWL + kc * 1024 + (c + 1) * 128],
                    lambda kc: B16[:, CBH + kc * 256:CBH + (kc + 1) * 256],
                    lambda kc: B16[:, CBL + kc * 256:CBL + (kc + 1) * 256],
                    list(range(8)))
                td = s.done("tensor", "T")
                s.wait("vector", "T", td)
                v(lambda E, c=c, bank=bank, ob=ob: E.tensor_scalar(
                    CS[:, 0:GT], PS[bank][:, 0:GT], SML[:, ob + c:ob + c + 1], None, ALU.add))
                v(lambda E: E.drain())
                v(lambda E, c=c, t0=t0: E.tensor_tensor(
                    CS[:, 0:GT], CS[:, 0:GT], B16[:, X1H + c * 2048 + t0:X1H + c * 2048 + t0 + GT], ALU.add))
                v(lambda E: E.drain())
                v(lambda E, c=c, t0=t0: E.tensor_tensor(
                    CS[:, 0:GT], CS[:, 0:GT], B16[:, X1L + c * 2048 + t0:X1L + c * 2048 + t0 + GT], ALU.add))
                v(lambda E: E.drain())
                v(lambda E, c=c: E.tensor_copy(FA[:, FXAD + c * GT:FXAD + (c + 1) * GT], CS[:, 0:GT]))
                vd = s.done("vector", "V")
                s.wait("tensor", "V", vd)
            vd = s.done("vector", "V")
            s.wait("scalar", "V", vd)
            for c in range(8):
                sc(lambda E, c=c: E.activation(CS[:, 256:256 + GT],
                                               FA[:, FXAD + c * GT:FXAD + (c + 1) * GT], ACT.Square))
                cd = s.done("scalar", "C")
                s.wait("tensor", "C", cd)
                pe(lambda E, c=c: E.matmul(PS[4][0:1, 0:GT], ONESC[:, :], CS[:, 256:256 + GT],
                                           start=(c == 0), stop=(c == 7)))
                td = s.done("tensor", "T")
                s.wait("scalar", "T", td)
            td = s.done("tensor", "T")
            s.wait("vector", "T", td)
            v(lambda E: E.tensor_copy(ROWS[0:1, 0:GT], PS[4][0:1, 0:GT]))
            v(lambda E: E.drain())
            v(lambda E: E.tensor_scalar(ROWS[0:1, 0:GT], ROWS[0:1, 0:GT], 1.0 / D, None, ALU.mult))
            vd = s.done("vector", "V")
            s.wait("scalar", "V", vd)
            sc(lambda E: E.activation(ROWS[0:1, 0:GT], ROWS[0:1, 0:GT], ACT.Sqrt, bias=EPSC[0:1, 1:2]))
            cd = s.done("scalar", "C")
            s.wait("vector", "C", cd)
            v(lambda E: E.reciprocal(ROWS[0:1, 256:256 + GT], ROWS[0:1, 0:GT]))
            vd = s.done("vector", "V")
            s.wait("tensor", "V", vd)
            pe(lambda E: E.matmul(PS[4][:, 0:GT], ONES1[:, :], ROWS[0:1, 256:256 + GT], start=True, stop=True))
            td = s.done("tensor", "T")
            s.wait("vector", "T", td)
            for c in range(8):
                v(lambda E, c=c: E.tensor_tensor(
                    FA[:, FXAD + c * GT:FXAD + (c + 1) * GT],
                    FA[:, FXAD + c * GT:FXAD + (c + 1) * GT], PS[4][:, 0:GT], ALU.mult))
                v(lambda E: E.drain())
                v(lambda E, c=c, nw=nw: E.tensor_scalar(
                    FA[:, FXAD + c * GT:FXAD + (c + 1) * GT],
                    FA[:, FXAD + c * GT:FXAD + (c + 1) * GT], SML[:, nw + c:nw + c + 1], None, ALU.mult))
                v(lambda E: E.drain())
                v(lambda E, c=c: E.tensor_copy(B16[:, HHI + c * 256:HHI + (c + 1) * 256],
                                               FA[:, FXAD + c * GT:FXAD + (c + 1) * GT]))
                v(lambda E: E.drain())
                v(lambda E, c=c: E.tensor_tensor(B16[:, HLO + c * 256:HLO + (c + 1) * 256],
                                                 FA[:, FXAD + c * GT:FXAD + (c + 1) * GT],
                                                 B16[:, HHI + c * 256:HHI + (c + 1) * 256], ALU.subtract))
                v(lambda E: E.drain())
            vd = s.done("vector", "V")
            s.wait("tensor", "V", vd)
            for c2 in range(2):
                bank = c2 % 2
                mm3(PS[bank][:, 0:GT],
                    lambda kc, c2=c2: B16[:, FDH + kc * 256 + c2 * 128:FDH + kc * 256 + c2 * 128 + 128],
                    lambda kc, c2=c2: B16[:, FDL + kc * 256 + c2 * 128:FDL + kc * 256 + c2 * 128 + 128],
                    lambda kc: B16[:, HHI + kc * 256:HHI + (kc + 1) * 256],
                    lambda kc: B16[:, HLO + kc * 256:HLO + (kc + 1) * 256],
                    list(range(8)))
                td = s.done("tensor", "T")
                s.wait("scalar", "T", td)
                sc(lambda E, bank=bank, c2=c2, fb=fb: E.activation(
                    CS[:, 512 + c2 * 256:512 + (c2 + 1) * 256], PS[bank][:, 0:GT], ACT.Gelu,
                    bias=SML[:, fb + c2:fb + c2 + 1]))
                cd = s.done("scalar", "C")
                s.wait("vector", "C", cd)
                v(lambda E, c2=c2: E.tensor_copy(GHL[:, c2 * 256:(c2 + 1) * 256],
                                                 CS[:, 512 + c2 * 256:512 + (c2 + 1) * 256]))
                v(lambda E: E.drain())
                v(lambda E, c2=c2: E.tensor_tensor(GHL[:, 512 + c2 * 256:512 + (c2 + 1) * 256],
                                                   CS[:, 512 + c2 * 256:512 + (c2 + 1) * 256],
                                                   GHL[:, c2 * 256:(c2 + 1) * 256], ALU.subtract))
                vd = s.done("vector", "V")
                s.wait("tensor", "V", vd)
            for c in range(8):
                bank = c % 2
                dpar = c % 2
                s.wait("sync", "G", state["spill"])
                s.wait("sync", "V", xp_used[dpar])
                xpl = s.dma("sync", FA[:, FXPJ + dpar * 256:FXPJ + dpar * 256 + 256],
                            T["xpost"][:, c * 2048 + t0:c * 2048 + t0 + GT], "L")
                mm3(PS[bank][:, 0:GT],
                    lambda kc, c=c: B16[:, FUH + kc * 1024 + c * 128:FUH + kc * 1024 + (c + 1) * 128],
                    lambda kc, c=c: B16[:, FUL + kc * 1024 + c * 128:FUL + kc * 1024 + (c + 1) * 128],
                    lambda kc: GHL[:, kc * 256:(kc + 1) * 256],
                    lambda kc: GHL[:, 512 + kc * 256:512 + (kc + 1) * 256],
                    list(range(2)))
                td = s.done("tensor", "T")
                s.wait("vector", "T", td)
                s.wait("vector", "L", xpl)
                v(lambda E, c=c, bank=bank, ub=ub: E.tensor_scalar(
                    CS[:, 0:GT], PS[bank][:, 0:GT], SML[:, ub + c:ub + c + 1], None, ALU.add))
                v(lambda E: E.drain())
                v(lambda E, c=c: E.tensor_tensor(CS[:, 0:GT], CS[:, 0:GT],
                                                 B16[:, HHI + c * 256:HHI + (c + 1) * 256], ALU.add))
                v(lambda E: E.drain())
                v(lambda E, c=c: E.tensor_tensor(CS[:, 0:GT], CS[:, 0:GT],
                                                 B16[:, HLO + c * 256:HLO + (c + 1) * 256], ALU.add))
                v(lambda E: E.drain())
                v(lambda E, c=c, dpar=dpar: E.tensor_tensor(
                    FA[:, FXAD + c * GT:FXAD + (c + 1) * GT], CS[:, 0:GT],
                    FA[:, FXPJ + dpar * 256:FXPJ + dpar * 256 + 256], ALU.add))
                vd = s.done("vector", "V")
                s.wait("tensor", "V", vd)
                xp_used[dpar] = vd
            emit_ln(FXAD, CS, 256, 0)
            for th in range(2):
                for half in range(2):
                    for cq in range(4):
                        c = half * 4 + cq
                        pe(lambda E, c=c, cq=cq, th=th: E.transpose(
                            PS[6][:, cq * 128:(cq + 1) * 128],
                            FA[:, FXAD + c * GT + th * 128:FXAD + c * GT + th * 128 + 128],
                            ID32[:, :]))
                    td = s.done("tensor", "T")
                    s.wait("vector", "T", td)
                    v(lambda E: E.tensor_copy(CS[:, 0:512], PS[6][:, 0:512]))
                    vd = s.done("vector", "V")
                    s.wait("tensor", "V", vd)
                    s.wait("gpsimd", "V", vd)
                    gv = s.dma("gpsimd", T["out"][t0 + th * 128:t0 + th * 128 + 128,
                                                  half * 512:half * 512 + 512], CS[:, 0:512], "G")
                    s.wait("vector", "G", gv)

        if "C" in phases:
            for sg in range(S // SGT):
                sg0 = sg * SGT
                gate_t = s.cnt["T"]
                c_prefetch(0, 0, 0, gate_t)
                seq = [(e, oh) for e in range(8) for oh in range(2)]
                for idx, (e, oh) in enumerate(seq):
                    par = idx % 2
                    if idx + 1 < len(seq):
                        c_prefetch(seq[idx + 1][0], seq[idx + 1][1], par ^ 1, gate_t)
                    if sg == 0 and idx == 0:
                        s.wait("tensor", "V", state["x1_done"])
                    emit_cexp(sg0, e, oh, par)
                if "D" in phases:
                    dgate = s.cnt["T"]
                    ow_t = tload(OWH, OWL, "outw", 1024, 1024, slab_done.get("outw", 0),
                                 after_t=dgate)
                    fd_t = tload(FDH, FDL, "ffdw", 256, 1024, slab_done.get("ffdw", 0),
                                 after_t=dgate)
                    fu_t = tload(FUH, FUL, "ffuw", 1024, 256, slab_done.get("ffuw", 0),
                                 after_t=dgate)
                    s.wait("tensor", "L", max(ow_t, fd_t, fu_t))
                    for dg in range(SGT // GT):
                        emit_dgroup(sg0, dg)

        if taps:
            vd = s.done("vector", "V")
            cd = s.done("scalar", "C")
            td = s.done("tensor", "T")
            pd = s.done("gpsimd", "P")
            s.wait("gpsimd", "V", vd)
            s.wait("gpsimd", "C", cd)
            s.wait("gpsimd", "T", td)
            for (tn, ap_fn, shp, dt) in taps:
                s.dma("gpsimd", tap_dram[tn], ap_fn(T), "G")

        s.wait("gpsimd", "G", s.cnt["G"])

        with nc.Block() as block:
            @block.sync
            def _(sync):
                for fn in s.ops["sync"]:
                    fn(sync)

            @block.tensor
            def _(tensor):
                for fn in s.ops["tensor"]:
                    fn(tensor)

            @block.vector
            def _(vector):
                for fn in s.ops["vector"]:
                    fn(vector)

            @block.scalar
            def _(scalar):
                for fn in s.ops["scalar"]:
                    fn(scalar)

            @block.gpsimd
            def _(gpsimd):
                for fn in s.ops["gpsimd"]:
                    fn(gpsimd)
    return nc


def kernel(L0_attn_x, L0_attn_Wq, L0_attn_Wk, L0_attn_Wv, L0_attn_Wo,
           L0_ffn_dw_kernel, L0_ffn_dw_bias, L0_ffn_pw_w, L0_ffn_pw_b,
           L0_ffn_router_w, L0_ffn_router_b, L0_ffn_expert_w, L0_ffn_expert_b,
           L0_ffn_out_w, L0_ffn_out_b, L0_ffn_norm_w, L0_ffn_ff_down_w,
           L0_ffn_ff_down_b, L0_ffn_ff_up_w, L0_ffn_ff_up_b):
    consts = host_consts()
    nc = build_program()
    x = np.ascontiguousarray(np.asarray(L0_attn_x, dtype=np.float32))
    weights = {
        "wq": L0_attn_Wq, "wk": L0_attn_Wk, "wv": L0_attn_Wv, "wo": L0_attn_Wo,
        "dwk": np.asarray(L0_ffn_dw_kernel, dtype=np.float32).reshape(D, 3),
        "dwb": L0_ffn_dw_bias, "pww": L0_ffn_pw_w, "pwb": L0_ffn_pw_b,
        "rww": L0_ffn_router_w, "rwb": L0_ffn_router_b,
        "expw": np.asarray(L0_ffn_expert_w, dtype=np.float32).reshape(8 * D, D),
        "expb": L0_ffn_expert_b, "outw": L0_ffn_out_w, "outb": L0_ffn_out_b,
        "normw": L0_ffn_norm_w, "ffdw": L0_ffn_ff_down_w, "ffdb": L0_ffn_ff_down_b,
        "ffuw": L0_ffn_ff_up_w, "ffub": L0_ffn_ff_up_b,
    }
    weights = {k: np.ascontiguousarray(np.asarray(vv, dtype=np.float32)) for k, vv in weights.items()}
    in_maps = []
    for b in range(8):
        m = {"x": np.ascontiguousarray(x[b])}
        m.update(weights)
        m.update(consts)
        in_maps.append(m)
    res = run_bass_kernel_spmd(nc, in_maps, core_ids=list(range(8)))
    return np.stack([np.asarray(r["out"]) for r in res.results], axis=0)


# version 02
# speedup vs baseline: 1.1669x; 1.1669x over previous
import base64
import contextlib
import numpy as np
import ml_dtypes

import concourse.bass as bass
import concourse.mybir as mybir
from concourse.bass_utils import run_bass_kernel_spmd

F32 = mybir.dt.float32
BF16 = mybir.dt.bfloat16
ALU = mybir.AluOpType
ACT = mybir.ActivationFunctionType
AXX = mybir.AxisListType.X

S, D = 2048, 1024
NH, HD = 16, 64
CH = 128
GT = 256
NG = S // GT
BT = 512
SGT = 1024
EPS_LN, EPS_RMS = 1e-5, 1e-6

_EXPTAB_B64 = "AACAP4Bz/D6Y83g+HYD1Pd8Ycj21ve48d25rPPEq6DsI82Q7isbhOkilXjoqj9s5/oNYOaGD1TjcjVI4pqLPN8zBTDcp68k2mB5HNupbxDUUo0E16PO+NEJOPDT0sbkzAh83MxuVtDJNFDIyS5yvMQwtLTGHxqowcWgoMNYSpi9wxSMvNoChLh5DHy7lDZ0tlOAaLeu6mCzgnBYsa4aUK013EiuPb5Aq9G4OKnR1jCkIgwopc5eIKK+yBijE1IQnWP0CJ3YsgSYKxP4l3zt7JXrA9yQzUXQkHe7wIwyXbSPVS+oiiQxnIorY4yHpr2AhfZLdICCAWiDfeNcfKHxUHwuK0R5iok4eCMXLHQzySB3jKMYcnWlDHBa0wBsuCD4b72W7GtrMOBr+PLYZOrYzGW04sRijwy4YZVesF8DzKReWmKcWyUUlFmP7ohX2uCAVjn6eFA5MHBRaIZoTff4XEw/jlRIbzxMSh8KRETm9DxEXv40QB8gLEDbYiQ//7gcPkQyGDtIwBA6tW4INCI0ADZqJfQzKBfoLdI52C2wj8wr8xG8KCnLsCeMqaQlf7+UIVL9iCJia3wcFgVwHcnLZBrhuVgayddMFoIdQBY2jzQS8yUoECfrHA1A0RQNueMICQMY/AqQdvQF2fjoBl+i3AAAAAAAAAAAAAAAAAAAAAAAAAAAAAACAP5hFGz+yWrw+PHxkPlWVCj4uHKg9hu1LPYFg9zyuCpY8ggI2PP/J3DtT6oU7kHIiOwwPxTpdC2863fwQOgjhrzkgWlU5kWcBOcX5nDjOaz44E/7mN6EajDds9Ck3YirONmoXejYqsBc2zAG4NTg2XzWCYgc15TqkNJw4RzTeqvEzJ5SSMxjPMTN0sdcyFNOCMryyHjK4gsAx9oZpMTOkDTHY0aswhm1QMPPV/C9GWpkvsgY6Lz+p4S7S3oguPAgmLj9oyS29UXQt6C8ULZXCsyxdD1oslUIELIlwoCuBn0Ir9xbsKhYyjyqBtC0qBrfSKTKcfykPCRspQxG8KCgjZChOXwoopNqnJwaeSycPAPcmL9CVJo27NSbrc9wlHbaFJTozIiU5wsQkK65uJFfEECR2nK8j8gZVIx41ASOSvJwikCE+IgWk5iEC5IshKrIpIQLazSDqtXkgB3UXIA+6tx8y314fui0HH976ox7x6kYepkzxHQJbkh3HiTEdXl3XHBSgghzddB4cqjfAG+sraRv6bA0b3I6rGkQcUBphc/wZfR6ZGSu+ORlFUeEYdamIGILHJRi6GckXffJzFyP2ExeAfLMWWrpZFgUPBBb8MaAVoVNCFey66xRC+o4UyXAtFOBk0hOLOH8TnswaE/HHuxI3ymMSXCkKEjOZpxGkTksRAACAP6DCMz+Ac/w+sESxPpjzeD6Yzy4+HYD1PTdjrD3fGHI9c/8pPbW97jwopKc8d25rPDxRJTzxKug7igajOwjzZDv/wyA7isbhOnSJnjpIpV4611YcOiqP2zkCLJo5/oNYOeMIGDmhg9U4We2VONyNUjhK2RM4pqLPN5vMkTfMwUw3KccPNynryTbtyI02mB5HNsPRCzbqW8Q1k+GJNRSjQTVD+Ac16PO+NLQVhjRCTjw03TkENPSxuTOoZIIzAh83M+OVADMblbQyEpt9Mk0UMjIjF/oxS5yvMZCfdjEMLS0xaTTzMIfGqjBF1W8wcWgoMBmC7C/WEqYv1jppL3DFIy8a/+UuNoChLvPOYi4eQx8uAKrfLeUNnS02kFwtlOAaLYiB2Szrupgsmn1WLOCcFix4hNMra4aUK8qVUCtNdxIrhLHNKo9vkCqb10oq9G4OKrYHyCl0dYwp5kFFKQiDCinUhcIoc5eIKHbTPyivsgYoqyq9J8TUhCd+izonWP0CJ3H1tyZ2LIEmkWg1JgrE/iW+5LIl3zt7JdlpMCV6wPck7PetJDNRdCSBjiskHe7wI6UtqSMMl20jOtUmI9VL6iIihaQiiQxnImg9IiKK2OMhnf2fIemvYCHOxR0hfZLdIN2VmyAggFogsG0ZIN941x9RTZcfKHxUH1c0FR8LitEeAACAP31fRz+YRRs/B9rxPrJavD7CsJI+PHxkPszxMT5VlQo+jNvXPS4cqD2c7II9hu1LPbTRHj2BYPc8SqjAPK4KljyItGk8ggI2PNe/DTz/ydw7YPOrO1PqhTs0llA7kHIiO0sH/ToMD8U6M3iZOl0Lbzr/Kjo63fwQOknV4TkI4a85iPmIOSBaVTmjKCY5kWcBOY2PyTjF+Zw4a4F0OM5rPjjUTBQ4E/7mN6rlszehGow36zlaN2z0KTdkXAQ3YirONtiPoDZqF3o2fMVCNiqwFzYKRew1zAG4NQhOjzU4Nl81aNYtNYJiBzUl4NI05TqkNBTOfzScOEc0UCcbNN6q8TP3NbwzJ5SSM61PZDMYzzEzT3oKM3Sx1zJm+6cyFNOCMsLFSzK8sh4yQzD3MbiCwDFr7ZUx9oZpMQTfNTEzpA0x8Z7cMNjRqzA20IUwhm1QMOJSIjDz1fwvn+jEL0ZamS/A3G4vsgY6L5fgEC8/qeEuvL6vLtLeiC6FMFUuPAgmLlVOAS4/aMktKducLb1RdC2sRj4t6C8ULQjR5iyVwrMsT/+LLF0PWixI0ykslUIELC4CziuJcKArpeZ5K4GfQiuWkhcr9xbsKurdtyoWMo8qsQpfKoG0LSobSAcqBrfSKd4apCkynH8pwhFHKQ8JGym+e/EoQxG8KJF3kigoI2QoAACAPwCFVj+gwjM/FqIWP4Bz/D60i9M+sESxPoOLlD6Y83g++JxQPpjPLj5bfBI+HYD1PaS4zT03Y6w9gHSQPd8Ycj2R3ko9c/8pPdpzDj21ve48ng7IPCikpzxPeow8d25rPKVIRTw8USU8yYcKPPEq6DuHjMI7igajOyyciDsI82Q7GNo/O//DIDtbtwY7isbhOkAxvTp0iZ46R9mEOkilXjrakTo611YcOs8BAzoqj9s5vPu3OQIsmjnpMIE5/oNYOdBuNTnjCBg50Mz+OKGD1Tjz6rI4We2VOJZEezjcjVI4AnAwOErZEzjkyPc3pqLPN9T9rTebzJE3jll0N8zBTDdflCs3KccPN2n28DYp68k2eTOpNu3IjTY7n202mB5HNgTbJjbD0Qs29lPqNepbxDXiiqQ1k+GJNWIUZzUUo0E16kIiNUP4BzVV4OM06PO+NBYDoDS0FYY0prdgNEJOPDQ9yx003TkENB+a3TP0sbkzOZubM6hkgjPCh1ozAh83MwxzGTPjlQAzMIDXMhuVtDJ0UpcyEpt9Ml+DVDJNFDIyezkVMiMX+jFCkdExS5yvMeEnkzGQn3YxgKlOMQwtLTGeHRExaTTzMCbMyzCHxqowrBqPMEXVbzDe+EgwcWgoMMoeDTAZguwvnS/GL9YSpi8FKosv1jppL1hwQy9wxSMvAACAP1HrYT99X0c/MfIvP5hFGz/lBgk/B9rxPvBu1T6yWrw+2TimPsKwkj4xdIE+PHxkPjejST7M8TE+FgkdPlWVCj5GmfQ9jNvXPWJ+vj0uHKg9TFuUPZzsgj2cFGc9hu1LPTf3Mz200R49TCgMPYFg9zw1T9o8SqjAPAAFqjyuCpY8TmmEPIi0aTx/Pk48ggI2PIKfIDzXvw080C/6O//J3Dt92MI7YPOrO/a+lztT6oU7F1xsOzSWUDu/Ezg7kHIiOwNcDztLB/06/0vfOgwPxTpd5606M3iZOrdvhzpdC286uPRSOv8qOjrrSiQ63fwQOgnn/zlJ1eE5DEzHOQjhrzlzNps5iPmIOXHCcTkgWlU5VEg8OaMoJjl0ohI5kWcBOfNl5DiNj8k4cuCxOMX5nDjRh4o4a4F0OIDGVzjOaz44yQsoONRMFDjY3wI4E/7mN6PZyzeq5bM3N8KeN6EajDdhSHc36zlaN4CVQDds9Ck3DPwVN2RcBDe+nek2YirONsLwtTbYj6A2BLKNNmoXejZ3tFw2fMVCNpviKzYqsBc2RN0FNgpF7DXcgdA1zAG4NbhiojUITo81nu58NTg2XzXV+0Q1aNYtNTxpGTWCYgc1DfTuNCXg0jTYGLo05TqkNLnukDQUzn80Q79hNJw4RzTizy80UCcbNCzsCDTeqvEzAACAPwFYaj8AhVY/UF9EP6DCMz+yjSQ/FqIWP/DjCT+Ac/w+XBjnPrSL0z5spsE+sESxPrpFoj6Di5Q+lvqHPpjzeD4+5GM++JxQPjD3Pj6Yzy4+2QUgPlt8Ej4FGAY+HYD1PX674D2kuM09fFG8PTdjrD30zZ09gHSQPSY8hD3fGHI9851dPZHeSj0stTk9c/8pPe6dGz3acw4932YCPbW97jx7i9o8ng7IPCAitzwopKc8rHWZPE96jDwamIA8d25rPOaDVzylSEU8OJg0PDxRJTwUVRc8yYcKPHif/TvxKug7FYfUO4eMwjtTF7I7igajOwM8lTssnIg7aht6OwjzZDvYlFE7GNo/O1GfLzv/wyA7aioTO1u3BjvPo/Y6isbhOhatzjpAMb06DTCtOnSJnjooIJE6R9mEOoo4czpIpV46nM9LOtqROjpyySo611YcOh4dDzrPAQM6ZtnvOSqP2zlU/Mg5vPu3OVdrqDkCLJo5OSGNOekwgTk6hmw5/oNYORMzRjnQbjU5qBUmOeMIGDliLAs50Mz+OMw+6Tihg9U4tnPDOPPqsjhCyKM4We2VOHw+iTiWRHs4EANmONyNUjgMvkA4AnAwOAiDIThK2RM4cFcHOOTI9zfM0uI3pqLPNwwSvjfU/a0300WfN5vMkTckd4U3jll0N9qtXzfMwUw3AACAP2B9cD9R62E/QDtUP31fRz8qSzs/MfIvPzlJJT+YRRs/St0RP+UGCT+SuQA/B9rxPtky4z7wbtU+iIDIPrJavD5G8bA+2TimPrQmnD7CsJI+jc2JPjF0gT6oOHM+PHxkPl6kVj43o0k+xGs9PszxMT7VKSc+FgkdPm2FEz5VlQo+3y8CPkaZ9D18x+U9jNvXPYvHyj1ifr49xvOyPS4cqD3A7J09TFuUPT9eiz2c7II94vt1PZwUZz1+FFk9hu1LPY6SPz039zM95g8pPbTRHj1hMhU9TCgMPWyqAz2BYPc8oGPoPDVP2jwsFc08SqjAPCD8tDwABao89LefPK4Kljx+84w8TmmEPCTHeDyItGk8tItbPH8+TjyZv0E8ggI2PH77KjyCnyA8NOQWPNe/DTxFKQU80C/6O1oH6zv/ydw7gGnPO33YwjtiCrc7YPOrO2CIoTv2vpc7WI2OO1PqhTuGmns7F1xsOxcKXjs0llA7+PJDO78TODuq7Cw7kHIiO/WaGDsDXA87eKwGO0sH/TrCsu06/0vfOprE0ToMD8U6nh65Ol3nrToSXqM6M3iZOtkrkDq3b4c6IHZ+Ol0Lbzq5j2A6uPRSOr0sRjr/Kjo6fOMuOutKJDqyVho63fwQOhE0CDoJ5/857GXwOUnV4TmOJtQ5DEzHOeY4uzkI4a85AACAP8LudD8BWGo/cDZgPwCFVj/WPk0/UF9EP/7hOz+gwjM/KP0rP7KNJD+GcB0/FqIWP/oeED/w4wk/2+0DP4Bz/D6JifE+XBjnPr8a3T60i9M+dGbKPmymwT46R7k+sESxPsyaqT66RaI+zEGbPoOLlD6EH44+lvqHPqoZgj6Y83g+XDBuPj7kYz4VClo++JxQPiyYRz4w9z4+tbU2PpjPLj7nQCc+2QUgPtEaGT5bfBI+JScMPgUYBj72SwA+HYD1PRLj6j1+u+A9TATXPaS4zT3Z08Q9fFG8PU0ttD03Y6w9Wu+kPfTNnT16+5Y9gHSQPcM1ij0mPIQ9UQl9Pd8Ycj1+oWc9851dPUAJVD2R3ko9VxlCPSy1OT3hrTE9c/8pPQamIj3unRs9quMUPdpzDj1JSwg932YCPVSH+Ty1ve48eWvkPHuL2jzEGNE8ng7IPINovzwgIrc8VjevPCikpzzPZKA8rHWZPEfTkjxPeow8l2eGPBqYgDzPEXY8d25rPNxAYTzmg1c8tzJOPKVIRTw8wTw8OJg0PIPJLDw8USU8mysePBRVFzw6yhA8yYcKPJ6KBDx4n/07iKjyO/Eq6Dt1Id47FYfUO/ZWyzuHjMI7YiO6O1MXsjtTZKo7igajO0b6mzsDPJU7YsiOOyyciDtCtII7aht6O2RLbzsI82Q7AACAP6sfeD9gfXA/NRdpP1HrYT/p91o/QDtUP6izTT99X0c/Kz1BPypLOz/8hzU/MfIvP2OIKj85SSU/YjMgP5hFGz+gfhY/St0RP2tgDT/lBgk/oc8EP5K5AD9kh/k+B9rxPiJp6j7ZMuM+XjXcPvBu1T7g3c4+iIDIPlJVwj6yWrw+KY+2PkbxsD6ff6s+2TimPqIboT60Jpw+z1iXPsKwkj5hLY4+jc2JPi2QhT4xdIE+JvF6Pqg4cz76vGs+PHxkPp50XT5epFY+yQlQPjejST4Pb0M+xGs9PtSXNz7M8TE+QXgsPtUpJz40BSI+FgkdPjo0GD5thRM+gfsOPlWVCj7PUQY+3y8CPvVc/D1GmfQ9vhLtPXzH5T2ttd49jNvXPWQ30T2Lx8o9ZIrEPWJ+vj3/obg9xvOyPUxyrT0uHKg9GPCiPcDsnT3jEJk9TFuUPczKjz0/Xos9ixSHPZzsgj3Tyn094vt1PXFqbj2cFGc9jfhfPX4UWT21ZlI9hu1LPVSnRT2Okj89rK05PTf3Mz3AbS495g8pPVHcIz200R49zO4ZPWEyFT1DmxA9TCgMPWDYBz1sqgM9xDr/PIFg9zwXxO88oGPoPEI94Tw1T9o8vZfTPCwVzTzixcY8SqjAPN26ujwg/LQ8omqvPAAFqjzhyaQ89LefPPfNmjyuCpY8AACAP7pnej/C7nQ/aJRvPwFYaj/lOGU/cDZgPwJQWz8AhVY/zdRRP9Y+TT+Gwkg/UF9EP6YUQD/+4Ts/0sY3P6DCMz/n1C8/KP0rP+k6KD+yjSQ/DPUgP4ZwHT+u/xk/FqIWP1JXEz/6HhA/pfgMP/DjCT944AY/2+0DP70LAT+Ac/w+Fe/2PomJ8T4uQuw+XBjnPmwL4j6/Gt0+tkXYPrSL0z4m7M4+dGbKPhD6xT5spsE+/Gq9PjpHuT6iOrU+sESxPuhkrT7Mmqk+5OWlPrpFoj7XuZ4+zEGbPirdlz6Di5Q+b0yRPoQfjj5dBIs+lvqHPs8BhT6qGYI+joN+PpjzeD7CgnM+XDBuPr77aD4+5GM+O+lePhUKWj4xRlU++JxQPtENTD4smEc+ejtDPjD3Pj7Gyjo+tbU2Pnu3Mj6Yzy4+jv0qPudAJz4mmSM+2QUgPoyGHD7RGhk+OsIVPlt8Ej7LSA8+JScMPgMXCT4FGAY+yikDPvZLAD5V/Po9HYD1PZUi8D0S4+o97MDlPX674D0o0ts9TATXPVFR0j2kuM09qznJPdnTxD2ihsA9fFG8PeEzuD1NLbQ9Pj2wPTdjrD29nqg9Wu+kPZJUoT30zZ09EVuaPXr7lj3CrpM9gHSQPU1MjT3DNYo9gTCHPSY8hD1SWIE9UQl9PaCBdz3fGHI9AACAP/UHfD+rH3g/40Z0P2B9cD/kwmw/NRdpPxd6ZT9R62E/qmpeP+n3Wj/Yklc/QDtUP+zwUD+os00/PoNKP31fRz8ySEQ/Kz1BPzg+Pj8qSzs/0GM4P/yHNT+AtzI/MfIvP+A3LT9jiCo/j+MnPzlJJT83uSI/YjMgP4+3HT+YRRs/Vd0YP6B+Fj9UKRQ/St0RP12aDz9rYA0/Ti8LP+UGCT8M5wY/oc8EP4PAAj+SuQA/WHX9PmSH+T4IqfU+B9rxPiQa7j4iaeo+yMbmPtky4z4erd8+XjXcPmDL2D7wbtU+1x/SPuDdzj7XqMs+iIDIPsJkxT5SVcI+B1K/PrJavD4ib7k+KY+2Ppq6sz5G8bA+ADOuPp9/qz711qg+2TimPiGloz6iG6E+NpyePrQmnD70upk+z1iXPiAAlT7CsJI+jmqQPmEtjj4X+Ys+jc2JPp+qhz4tkIU+E36DPjF0gT7N5H4+JvF6Pi8Ndz6oOHM+VXNvPvq8az5aFWg+PHxkPmXxYD6edF0+rgVaPl6kVj55UFM+yQlQPhnQTD43o0k+74JGPg9vQz5mZ0A+xGs9Pvh7Oj7Ulzc+Kr80PszxMT6NLy8+QXgsPrzLKT7VKSc+YJIkPjQFIj4pgh8+FgkdPtOZGj46NBg+JdgVPm2FEz7tOxE+gfsOPgXEDD5VlQo+AACAP+gvfT+6Z3o/YKd3P8LudD/MPXI/aJRvP4DybD8BWGo/1MRnP+U4ZT8gtGI/cDZgP8K/XT8CUFs/HudYPwCFVj+WKVQ/zdRRP5OGTz/WPk0/gv1KP4bCSD/RjUY/UF9EP/I2Qj+mFEA/Wvg9P/7hOz+A0Tk/0sY3P+LBNT+gwjM//MgxP+fULz9Q5i0/KP0rP2AZKj/pOig/tGEmP7KNJD/UviI/DPUgP0wwHz+GcB0/q7UbP67/GT+BThg/FqIWP2D6FD9SVxM/37gRP/oeED+ViQ4/pfgMPx1sCz/w4wk/EmAIP3jgBj8UZQU/2+0DP8J6Aj+9CwE/gEH/PoBz/D5krfk+Fe/2Pn849D6JifE+IOLuPi5C7D6eqek+XBjnPlKO5D5sC+I+l4/fPr8a3T7QrNo+tkXYPl3l1T60i9M+qDjRPibszj4apsw+dGbKPiItyD4Q+sU+L83DPmymwT62hb8+/Gq9Pi5Wuz46R7k+ED63PqI6tT7cPLM+sESxPg9Srz7oZK0+LH2rPsyaqT65vac+5OWlPj4TpD66RaI+Rn2gPte5nj5e+5w+zEGbPhWNmT4q3Zc+/jGWPoOLlD6t6ZI+b0yRPruzjz6EH44+vo+MPl0Eiz5UfYk+lvqHPhl8hj7PAYU+rouDPqoZgj62q4A+joN+PqS3ez6Y83g+AACAP/8Bfj/1B3w/3BF6P6sfeD9bMXY/40Z0Pz1gcj9gfXA/RZ5uP+TCbD8362o/NRdpP9hGZz8XemU/7bBjP1HrYT89KWA/qmpeP5CvXD/p91o/rkNZP9iSVz9h5VU/QDtUP3KUUj/s8FA/rFBPP6izTT/bGUw/PoNKP8zvSD99X0c/TNJFPzJIRD8pwUI/Kz1BPzK8Pz84Pj4/OMM8PypLOz8I1jk/0GM4P3j0Nj/8hzU/Vh40P4C3Mj92UzE/MfIvP6uTLj/gNy0/yt4rP2OIKj+nNCk/j+MnPxeVJj85SSU/8P8jPze5Ij8KdSE/YjMgPzr0Hj+Ptx0/Wn0cP5hFGz9CEBo/Vd0YP8ysFz+gfhY/0FIVP1QpFD8pAhM/St0RP7K6ED9dmg8/R3wOP2tgDT/ERgw/Ti8LPwUaCj/lBgk/6PUHPwznBj9L2gU/oc8EPwvHAz+DwAI/B7wBP5K5AD8/cv8+WHX9Pmd8+z5kh/k+R5b3Pgip9T6gv/M+B9rxPjb47z4kGu4+yz/sPiJp6j4klug+yMbmPgb75D7ZMuM+OG7hPh6t3z6C790+XjXcPqp+2j5gy9g+ehvXPvBu1T68xdM+1x/SPjt90D7g3c4+wUHNPteoyz4cE8o+iIDIPhfxxj7CZMU+gtvDPlJVwj4r0sA+B1K/PuHUvT6yWrw+AACAP/aWfj/oL30/1sp7P7pnej+UBnk/YKd3PxtKdj/C7nQ/VJVzP8w9cj8p6HA/aJRvP4ZCbj+A8mw/VaRrPwFYaj+BDWk/1MRnP/Z9Zj/lOGU/nvVjPyC0Yj9mdGE/cDZgPzr6Xj/Cv10/BodcPwJQWz+2Glo/HudYPze1Vz8AhVY/dVZVP5YpVD9e/lI/zdRRP+CsUD+Thk8/5mFOP9Y+TT9gHUw/gv1KPzrfST+Gwkg/ZKdHP9GNRj/LdUU/UF9EP15KQz/yNkI/CiVBP6YUQD/ABT8/Wvg9P27sPD/+4Ts/BNk6P4DROT9wyzg/0sY3P6TDNj/iwTU/jME0P6DCMz8cxTI//MgxP0HOMD/n1C8/7NwuP1DmLT8P8Sw/KP0rP5kKKz9gGSo/eykpP+k6KD+nTSc/tGEmPw53JT+yjSQ/n6UjP9S+Ij9O2SE/DPUgPwwSID9MMB8/y08eP4ZwHT98khw/q7UbPxHaGj+u/xk/fiYZP4FOGD+0dxc/FqIWP6XNFT9g+hQ/RSgUP1JXEz+GhxI/37gRP1vrED/6HhA/uFMPP5WJDj+QwA0/pfgMP9UxDD8dbAs/fKcKP/DjCT94IQk/EmAIP72fBz944AY/PyIGPxRlBT/zqAQ/2+0DP8szAz/CegI/vcIBP70LAT++VQA/gEH/PoLZ/T6Ac/w+AACAP4AAfz//AX4/fAR9P/UHfD9rDHs/3BF6P0cYeT+rH3g/CCh3P1sxdj+kO3U/40Z0PxZTcz89YHI/Vm5xP2B9cD9bjW8/RZ5uPx6wbT/kwmw/mNZrPzfraj/BAGo/NRdpP5IuaD/YRmc/BGBmPxd6ZT8QlWQ/7bBjP67NYj9R62E/1wlhPz0pYD+ESV8/qmpeP66MXT+Qr1w/T9NbP+n3Wj9fHVo/rkNZP9dqWD/Yklc/sbtWP2HlVT/mD1U/QDtUP29nUz9ylFI/RsJRP+zwUD9kIFA/rFBPP8KBTj+os00/W+ZMP9sZTD8nTks/PoNKPyC5ST/M70g/QCdIP31fRz+BmEY/TNJFP9wMRT8ySEQ/TIRDPynBQj/J/kE/Kz1BP058QD8yvD8/1vw+Pzg+Pj9ZgD0/OMM8P9IGPD8qSzs/PJA6PwjWOT+QHDk/0GM4P8irNz949DY/3j02P/yHNT/O0jQ/Vh40P5FqMz+AtzI/IgUyP3ZTMT97ojA/MfIvP5ZCLz+rky4/b+UtP+A3LT//iiw/yt4rP0EzKz9jiCo/MN4pP6c0KT/Giyg/j+MnP/87Jz8XlSY/1e4lPzlJJT9CpCQ/8P8jP0JcIz83uSI/0BYiPwp1IT/l0yA/YjMgP36THz869B4/llUeP4+3HT8mGh0/Wn0cPyvhGz+YRRs/"


def _tab():
    return np.frombuffer(base64.b64decode(_EXPTAB_B64), dtype=np.float32).reshape(16, 129)


def host_consts():
    tab = _tab()
    t = np.arange(S, dtype=np.float64)
    inv = 1.0 / (10000.0 ** (np.arange(0, HD, 2, dtype=np.float64) / HD))
    emb = np.concatenate([np.outer(t, inv)] * 2, axis=-1)
    cos32 = np.cos(emb).astype(np.float32)
    sin32 = np.sin(emb).astype(np.float32)
    p = np.arange(128)
    pm = p % 64
    c_cos = np.ascontiguousarray(cos32[:, pm].T)
    sgn = np.where(pm < 32, -1.0, 1.0).astype(np.float32)
    c_sin = np.ascontiguousarray((sin32[:, pm] * sgn[None, :]).T)
    tt = np.arange(CH)
    c_mask = np.zeros((128, NH * CH), dtype=np.float32)
    for h in range(NH):
        dd = tt[None, :] - tt[:, None]
        c_mask[:, h * CH:(h + 1) * CH] = np.where(dd >= 0, tab[h][np.maximum(dd, 0)], np.float32(0))
    c_qdec = np.zeros((64, NH * CH), dtype=np.float32)
    for h in range(NH):
        c_qdec[:, h * CH:(h + 1) * CH] = tab[h][1:129][None, :]
    c_kdecc = np.zeros((128, NH), dtype=np.float32)
    for h in range(NH):
        c_kdecc[:, h] = tab[h][127 - tt]
    c_prot = np.zeros((128, 128), dtype=np.float32)
    c_prot[p, p ^ 32] = 1.0
    return {
        "c_cos": c_cos, "c_sin": c_sin, "c_mask": c_mask, "c_qdec": c_qdec,
        "c_kdecc": c_kdecc,
        "c_id32": np.eye(128, dtype=np.float32),
        "c_prot": c_prot,
        "c_ones1": np.ones((1, 128), dtype=np.float32),
        "c_onesc": np.ones((128, 1), dtype=np.float32),
        "c_eps": np.array([[EPS_LN, EPS_RMS]], dtype=np.float32),
    }


class Sched:
    ENGS = ("sync", "tensor", "vector", "scalar", "gpsimd")

    def __init__(self, nc, es):
        self.nc = nc
        self.ops = {e: [] for e in self.ENGS}
        self.sem = {k: es.enter_context(nc.semaphore("sem" + k)) for k in "LGTVCP"}
        self.cnt = {k: 0 for k in "LGTVCP"}

    def emit(self, eng, fn):
        self.ops[eng].append(fn)

    def wait(self, eng, s_, val):
        if val <= 0:
            return
        sem = self.sem[s_]
        self.emit(eng, lambda E, sem=sem, val=val: E.wait_ge(sem, val))

    def dma(self, eng, out, in_, s_, transpose=False, slow=False):
        self.cnt[s_] += 16
        sem = self.sem[s_]
        if transpose:
            self.emit(eng, lambda E, out=out, in_=in_, sem=sem:
                      E.dma_start_transpose(out=out, in_=in_).then_inc(sem, 16))
        elif slow:
            self.emit(eng, lambda E, out=out, in_=in_, sem=sem:
                      E.dma_start(out=out, in_=in_, allow_slow_non_contiguous=True).then_inc(sem, 16))
        else:
            self.emit(eng, lambda E, out=out, in_=in_, sem=sem:
                      E.dma_start(out=out, in_=in_).then_inc(sem, 16))
        return self.cnt[s_]

    def done(self, eng, s_):
        self.cnt[s_] += 1
        sem = self.sem[s_]

        def _f(E, sem=sem):
            E.drain()
            E.nop().then_inc(sem, 1)
        self.emit(eng, _f)
        return self.cnt[s_]


def build_program(phases=("0", "A", "B", "C", "D"), taps=()):
    nc = bass.Bass("TRN2", target_bir_lowering=False)
    T = {}

    def din(name, shape, dt=F32):
        T[name] = nc.dram_tensor(name, shape, dt, kind="ExternalInput").ap()
        return T[name]

    din("x", [S, D])
    din("wq", [D, D]); din("wk", [256, D]); din("wv", [256, D]); din("wo", [D, D])
    din("dwk", [D, 3]); din("dwb", [D]); din("pww", [D, D]); din("pwb", [D])
    din("rww", [8, D]); din("rwb", [8])
    din("expw", [8 * D, D]); din("expb", [8, D])
    din("outw", [D, D]); din("outb", [D]); din("normw", [D])
    din("ffdw", [256, D]); din("ffdb", [256]); din("ffuw", [D, 256]); din("ffub", [D])
    for n, shp, dt in [("c_cos", [128, S], F32), ("c_sin", [128, S], F32),
                       ("c_mask", [128, 2048], F32), ("c_qdec", [64, 2048], F32),
                       ("c_kdecc", [128, 16], F32),
                       ("c_id32", [128, 128], F32),
                       ("c_prot", [128, 128], F32), ("c_ones1", [1, 128], F32),
                       ("c_onesc", [128, 1], F32), ("c_eps", [1, 2], F32)]:
        din(n, shp, dt)
    T["out"] = nc.dram_tensor("out", [S, D], F32, kind="ExternalOutput").ap()

    BOUNCE = [("x", [S, D]), ("wq", [D, D]), ("wk", [256, D]), ("wv", [256, D]),
              ("wo", [D, D]), ("pww", [D, D]), ("expw", [8 * D, D]),
              ("outw", [D, D]), ("ffdw", [256, D]), ("ffuw", [D, 256]),
              ("rww", [8, D])]
    for n, shp in BOUNCE:
        T[n + "_hi"] = nc.dram_tensor(n + "_hi", shp, BF16, kind="Internal").ap()
        T[n + "_lo"] = nc.dram_tensor(n + "_lo", shp, BF16, kind="Internal").ap()
    T["xpost"] = nc.dram_tensor("xpost", [128, 16384], F32, kind="Internal").ap()
    T["rw_dram"] = nc.dram_tensor("rw_dram", [8, 2048], F32, kind="Internal").ap()

    tap_dram = {}
    for (tn, _fn, shp, dt) in taps:
        tap_dram[tn] = nc.dram_tensor("tap_" + tn, shp, dt, kind="ExternalOutput").ap()

    es = contextlib.ExitStack()
    with es:
        def sbt(name, shape, dt):
            T[name] = es.enter_context(nc.sbuf_tensor(name, shape, dt))
            return T[name]

        B16 = sbt("B16", [128, 65536], BF16)
        FA = sbt("FA", [128, 11264], F32)
        KVS = sbt("KVS", [64, 1024], F32)
        VTM = sbt("VTM", [128, 512], F32)
        KDK = sbt("KDK", [128, 1024], F32)
        QDEC = sbt("QDEC", [64, 2048], F32)
        CS = sbt("CS", [128, 1024], F32)
        ROWS = sbt("ROWS", [1, 512], F32)
        SML = sbt("SML", [128, 272], F32)
        RT = sbt("RT", [128, 1536], F32)
        LGS = sbt("LGS", [128, 64], F32)
        RTW = sbt("RTW", [128, 128], BF16)
        GHL = sbt("GHL", [128, 1024], BF16)
        ID32 = sbt("ID32", [128, 128], F32)
        PROT = sbt("PROT", [128, 128], F32)
        ONES1 = sbt("ONES1", [1, 128], F32)
        ONESC = sbt("ONESC", [128, 1], F32)
        EPSC = sbt("EPSC", [1, 2], F32)
        PS = [es.enter_context(nc.psum_tensor("ps%d" % i, [128, 512], F32)) for i in range(8)]

        s = Sched(nc, es)

        def v(fn):
            s.emit("vector", fn)

        def sc(fn):
            s.emit("scalar", fn)

        def pe(fn):
            s.emit("tensor", fn)

        def gp(fn):
            s.emit("gpsimd", fn)

        WQH, WQL = 0, 8192
        WKH, WKL = 16384, 18432
        WVH, WVL = 20480, 22528
        WOH, WOL = 24576, 32768
        XFP = [40960, 45056]
        OFH, OFL = 49152, 51200
        PWH, PWL = 0, 8192
        X1H, X1L = 16384, 32768
        XCH, XCL = 49152, 53248
        RWST = 61440
        EWP = [0, 8192]
        OWH, OWL = 0, 8192
        FDH, FDL = 49152, 51200
        FUH, FUL = 53248, 55296
        CBH, CBL = 57344, 59392
        HHI, HLO = 61440, 63488
        FSTS = 0
        FQ64 = 2048
        FK64 = 6144
        FQDQ = 7168
        FMASK = 9216
        FXB = [0, 4128]
        FRW8 = 9216
        FCMB = 0
        FSCR = 8192
        FXAD = 8192
        FXPJ = 10240

        def mm3(psl, wH, wL, xH, xL, kcs):
            n = len(kcs)
            for i, kc in enumerate(kcs):
                a, b_, c_, d_ = wH(kc), xH(kc), wL(kc), xL(kc)
                pe(lambda E, psl=psl, a=a, b=b_, st=(i == 0): E.matmul(psl, a, b, start=st, stop=False))
                pe(lambda E, psl=psl, a=a, d=d_: E.matmul(psl, a, d, start=False, stop=False))
                pe(lambda E, psl=psl, c=c_, b=b_, sp=(i == n - 1): E.matmul(psl, c, b, start=False, stop=sp))

        bias_col = {}
        consts_done = 0
        store_par = [0, 0]
        slab_done = {}
        state = {"spill": 0, "x1_done": 0}

        def emit_slab(nm, r0, r, C):
            par = emit_slab.par
            emit_slab.par ^= 1
            src = T[nm][r0:r0 + r, 0:C]
            wt = FA[0:r, par * 1024:par * 1024 + C]
            hi = B16[0:r, 57344 + par * 2048:57344 + par * 2048 + C]
            lo = B16[0:r, 57344 + par * 2048 + 1024:57344 + par * 2048 + 1024 + C]
            s.wait("sync", "G", store_par[par])
            lv = s.dma("sync", wt, src, "L")
            s.wait("vector", "L", lv)
            v(lambda E, hi=hi, wt=wt: E.tensor_copy(hi, wt))
            v(lambda E: E.drain())
            v(lambda E, lo=lo, wt=wt, hi=hi: E.tensor_tensor(lo, wt, hi, ALU.subtract))
            dv = s.done("vector", "V")
            s.wait("gpsimd", "V", dv)
            s.dma("gpsimd", T[nm + "_hi"][r0:r0 + r, 0:C], hi, "G")
            g = s.dma("gpsimd", T[nm + "_lo"][r0:r0 + r, 0:C], lo, "G")
            store_par[par] = g
            slab_done[nm] = g
        emit_slab.par = 0

        if "0" in phases:
            for cn, dst in [("c_id32", ID32[:, :]),
                            ("c_prot", PROT[:, :]), ("c_ones1", ONES1[:, :]),
                            ("c_onesc", ONESC[:, :]), ("c_kdecc", SML[:, 248:264]),
                            ("c_eps", EPSC[:, :]),
                            ("c_qdec", QDEC[:, :]),
                            ("c_mask", FA[:, FMASK:FMASK + 2048])]:
                s.dma("sync", dst, T[cn], "L")
            col = 0
            for bn, wid, ap in [
                ("dwb", 8, T["dwb"].rearrange("(c p) -> p c", p=128)),
                ("pwb", 8, T["pwb"].rearrange("(c p) -> p c", p=128)),
                ("outb", 8, T["outb"].rearrange("(c p) -> p c", p=128)),
                ("normw", 8, T["normw"].rearrange("(c p) -> p c", p=128)),
                ("ffub", 8, T["ffub"].rearrange("(c p) -> p c", p=128)),
                ("ffdb", 2, T["ffdb"].rearrange("(c p) -> p c", p=128)),
                ("dwk", 24, T["dwk"].rearrange("(c p) k -> p k c", p=128)),
                ("expb", 64, T["expb"].rearrange("e (c p) -> p e c", p=128)),
            ]:
                bias_col[bn] = col
                dst = SML[:, col:col + wid]
                if bn == "dwk":
                    dst = dst.rearrange("p (k c) -> p k c", c=8)
                elif bn == "expb":
                    dst = dst.rearrange("p (e c) -> p e c", c=8)
                s.dma("sync", dst, ap, "L", slow=True)
                col += wid
            bias_col["rwb"] = 240
            s.dma("sync", SML[0:8, 240:241], T["rwb"].unsqueeze(1), "L", slow=True)
            consts_done = s.cnt["L"]

            slab_jobs = []
            for n, shp in BOUNCE:
                R, C = shp
                for r0 in range(0, R, 128):
                    slab_jobs.append((n, r0, min(128, R - r0), C))
            prio = {"x": 0, "wq": 1, "wk": 2, "wv": 3, "wo": 4}
            slab_jobs.sort(key=lambda j: prio.get(j[0], 5))
            for j in slab_jobs:
                emit_slab(*j)

        def tload(dst_h, dst_l, nm, Rw, Cw, after_g, after_t=0, after_v=0):
            s.wait("sync", "G", after_g)
            s.wait("sync", "T", after_t)
            s.wait("sync", "V", after_v)
            last = 0
            for kc in range(Cw // 128):
                o = kc * Rw
                last = s.dma("sync", B16[:, dst_h + o:dst_h + o + Rw],
                             T[nm + "_hi"][0:Rw, kc * 128:(kc + 1) * 128], "L", transpose=True)
                last = s.dma("sync", B16[:, dst_l + o:dst_l + o + Rw],
                             T[nm + "_lo"][0:Rw, kc * 128:(kc + 1) * 128], "L", transpose=True)
            return last

        def emit_ln(xab, sqbuf, sqoff, epscol):
            vd = s.done("vector", "V")
            s.wait("scalar", "V", vd)
            for c in range(8):
                sc(lambda E, c=c, xab=xab, sqbuf=sqbuf, sqoff=sqoff: E.activation(
                    sqbuf[:, sqoff:sqoff + GT], FA[:, xab + c * GT:xab + (c + 1) * GT], ACT.Square))
                cd = s.done("scalar", "C")
                s.wait("tensor", "C", cd)
                pe(lambda E, c=c, xab=xab: E.matmul(PS[4][0:1, 0:GT], ONESC[:, :],
                                                    FA[:, xab + c * GT:xab + (c + 1) * GT],
                                                    start=(c == 0), stop=(c == 7)))
                pe(lambda E, c=c, sqbuf=sqbuf, sqoff=sqoff: E.matmul(
                    PS[5][0:1, 0:GT], ONESC[:, :], sqbuf[:, sqoff:sqoff + GT],
                    start=(c == 0), stop=(c == 7)))
                td = s.done("tensor", "T")
                s.wait("scalar", "T", td)
            td = s.done("tensor", "T")
            s.wait("vector", "T", td)
            v(lambda E: E.tensor_copy(ROWS[0:1, 0:GT], PS[4][0:1, 0:GT]))
            v(lambda E: E.tensor_copy(ROWS[0:1, 256:256 + GT], PS[5][0:1, 0:GT]))
            v(lambda E: E.drain())
            v(lambda E: E.tensor_scalar(ROWS[0:1, 0:GT], ROWS[0:1, 0:GT], 1.0 / D, None, ALU.mult))
            v(lambda E: E.tensor_scalar(ROWS[0:1, 256:256 + GT], ROWS[0:1, 256:256 + GT], 1.0 / D, None, ALU.mult))
            v(lambda E: E.drain())
            v(lambda E: E.tensor_tensor(VTM[0:1, 256:256 + GT], ROWS[0:1, 0:GT], ROWS[0:1, 0:GT], ALU.mult))
            v(lambda E: E.drain())
            v(lambda E: E.tensor_tensor(ROWS[0:1, 256:256 + GT], ROWS[0:1, 256:256 + GT],
                                        VTM[0:1, 256:256 + GT], ALU.subtract))
            vd = s.done("vector", "V")
            s.wait("scalar", "V", vd)
            sc(lambda E, epscol=epscol: E.activation(ROWS[0:1, 256:256 + GT], ROWS[0:1, 256:256 + GT],
                                               ACT.Sqrt, bias=EPSC[0:1, epscol:epscol + 1]))
            cd = s.done("scalar", "C")
            s.wait("vector", "C", cd)
            v(lambda E: E.reciprocal(VTM[0:1, 0:GT], ROWS[0:1, 256:256 + GT]))
            vd = s.done("vector", "V")
            s.wait("tensor", "V", vd)
            pe(lambda E: E.matmul(PS[4][:, 0:GT], ONES1[:, :], ROWS[0:1, 0:GT], start=True, stop=True))
            pe(lambda E: E.matmul(PS[5][:, 0:GT], ONES1[:, :], VTM[0:1, 0:GT], start=True, stop=True))
            td = s.done("tensor", "T")
            s.wait("vector", "T", td)
            for c in range(8):
                v(lambda E, c=c, xab=xab: E.tensor_tensor(FA[:, xab + c * GT:xab + (c + 1) * GT],
                                                          FA[:, xab + c * GT:xab + (c + 1) * GT],
                                                          PS[4][:, 0:GT], ALU.subtract))
            v(lambda E: E.drain())
            for c in range(8):
                v(lambda E, c=c, xab=xab: E.tensor_tensor(FA[:, xab + c * GT:xab + (c + 1) * GT],
                                                          FA[:, xab + c * GT:xab + (c + 1) * GT],
                                                          PS[5][:, 0:GT], ALU.mult))
            vd = s.done("vector", "V")
            s.wait("tensor", "V", vd)
            return vd

        group_t_end = [0] * (NG + 2)
        wobank_vd = {0: 0, 1: 0}
        xf_ready = [0, 0]
        cs_ready = [0, 0]

        def prefetch_group(g):
            par = g % 2
            base = XFP[par]
            t0 = g * GT
            s.wait("sync", "G", slab_done.get("x", 0))
            s.wait("sync", "T", group_t_end[g - 2] if g >= 2 else 0)
            lv = 0
            for kc in range(8):
                lv = s.dma("sync", B16[:, base + kc * 256:base + (kc + 1) * 256],
                           T["x_hi"][t0:t0 + GT, kc * 128:(kc + 1) * 128], "L", transpose=True)
                lv = s.dma("sync", B16[:, base + 2048 + kc * 256:base + 2048 + (kc + 1) * 256],
                           T["x_lo"][t0:t0 + GT, kc * 128:(kc + 1) * 128], "L", transpose=True)
            xf_ready[par] = lv
            lv = s.dma("sync", CS[:, par * 512:par * 512 + 256], T["c_cos"][:, t0:t0 + GT], "L")
            lv = s.dma("sync", CS[:, par * 512 + 256:par * 512 + 512], T["c_sin"][:, t0:t0 + GT], "L")
            cs_ready[par] = lv

        def emit_groupA(g, wA):
            par = g % 2
            XH = XFP[par]
            XL = XFP[par] + 2048
            t0 = g * GT
            cs_cos = CS[:, par * 512:par * 512 + 256]
            cs_sin = CS[:, par * 512 + 256:par * 512 + 512]
            if g + 1 < NG:
                prefetch_group(g + 1)
            s.wait("tensor", "L", max(xf_ready[par], wA, consts_done))
            s.wait("vector", "L", max(xf_ready[par], cs_ready[par]))
            s.wait("vector", "G", state["spill"])

            def rope_crow(c, bank, dst64):
                qb = (c % 2) * 256
                tsin = 512 + (c % 2) * 256
                tcos = 1024
                psq = PS[bank][:, 0:GT]
                ps2 = PS[2][:, (c % 2) * 256:(c % 2) * 256 + GT]
                td = s.done("tensor", "T")
                s.wait("vector", "T", td)
                v(lambda E, qb=qb, psq=psq: E.tensor_copy(RT[:, qb:qb + GT], psq))
                vd = s.done("vector", "V")
                s.wait("tensor", "V", vd)
                pe(lambda E, ps2=ps2, qb=qb: E.matmul(ps2, PROT[:, :], RT[:, qb:qb + GT], start=True, stop=True))
                t2 = s.done("tensor", "T")
                s.wait("vector", "T", t2)
                v(lambda E, tsin=tsin, ps2=ps2, cs_sin=cs_sin: E.tensor_tensor(
                    RT[:, tsin:tsin + GT], ps2, cs_sin, ALU.mult))
                v2 = s.done("vector", "V")
                s.wait("gpsimd", "V", v2)
                gp(lambda E, tcos=tcos, qb=qb, cs_cos=cs_cos: E.tensor_tensor(
                    RT[:, tcos:tcos + GT], RT[:, qb:qb + GT], cs_cos, ALU.mult))
                gp(lambda E: E.drain())
                gp(lambda E, dst64=dst64, tcos=tcos, tsin=tsin: E.tensor_tensor(
                    FA[0:64, dst64:dst64 + GT], RT[0:64, tcos:tcos + GT],
                    RT[0:64, tsin:tsin + GT], ALU.add))
                gp(lambda E, dst64=dst64, tcos=tcos, tsin=tsin: E.tensor_tensor(
                    FA[0:64, dst64 + GT:dst64 + 2 * GT], RT[64:128, tcos:tcos + GT],
                    RT[64:128, tsin:tsin + GT], ALU.add))
                return s.done("gpsimd", "P")

            rope_p = 0
            for c in range(8):
                bank = c % 2
                mm3(PS[bank][:, 0:GT],
                    lambda kc, c=c: B16[:, WQH + kc * 1024 + c * 128:WQH + kc * 1024 + (c + 1) * 128],
                    lambda kc, c=c: B16[:, WQL + kc * 1024 + c * 128:WQL + kc * 1024 + (c + 1) * 128],
                    lambda kc, XH=XH: B16[:, XH + kc * 256:XH + (kc + 1) * 256],
                    lambda kc, XL=XL: B16[:, XL + kc * 256:XL + (kc + 1) * 256],
                    list(range(8)))
                rope_p = rope_crow(c, bank, FQ64 + (2 * c) * GT)
            for c in range(2):
                bank = c % 2
                mm3(PS[bank][:, 0:GT],
                    lambda kc, c=c: B16[:, WKH + kc * 256 + c * 128:WKH + kc * 256 + (c + 1) * 128],
                    lambda kc, c=c: B16[:, WKL + kc * 256 + c * 128:WKL + kc * 256 + (c + 1) * 128],
                    lambda kc, XH=XH: B16[:, XH + kc * 256:XH + (kc + 1) * 256],
                    lambda kc, XL=XL: B16[:, XL + kc * 256:XL + (kc + 1) * 256],
                    list(range(8)))
                rope_p = rope_crow(c, bank, FK64 + (2 * c) * GT)

            for chx in range(2):
                mm3(PS[3][:, 0:256],
                    lambda kc, XH=XH, chx=chx: B16[:, XH + kc * 256 + chx * 128:XH + kc * 256 + chx * 128 + 128],
                    lambda kc, XL=XL, chx=chx: B16[:, XL + kc * 256 + chx * 128:XL + kc * 256 + chx * 128 + 128],
                    lambda kc: B16[:, WVH + kc * 256:WVH + (kc + 1) * 256],
                    lambda kc: B16[:, WVL + kc * 256:WVL + (kc + 1) * 256],
                    list(range(8)))
                td = s.done("tensor", "T")
                s.wait("vector", "T", td)
                v(lambda E, chx=chx: E.tensor_copy(VTM[:, chx * 256:chx * 256 + 256], PS[3][:, 0:256]))
                vd = s.done("vector", "V")
                s.wait("tensor", "V", vd)

            s.wait("tensor", "P", rope_p)
            s.wait("vector", "P", rope_p)
            for chx in range(2):
                for gkv in range(4):
                    pe(lambda E, gkv=gkv, chx=chx: E.transpose(
                        PS[2][:, gkv * 64:(gkv + 1) * 64],
                        FA[0:64, FK64 + gkv * GT + chx * 128:FK64 + gkv * GT + chx * 128 + 128],
                        ID32[0:64, 0:64]))
                td = s.done("tensor", "T")
                s.wait("vector", "T", td)
                for hh in range(16):
                    v(lambda E, hh=hh: E.tensor_scalar(
                        KDK[:, hh * 64:(hh + 1) * 64], PS[2][:, (hh // 4) * 64:(hh // 4) * 64 + 64],
                        SML[:, 248 + hh:249 + hh], None, ALU.mult))
                v(lambda E: E.drain())
                v(lambda E, chx=chx: E.tensor_tensor(
                    FA[0:64, FQDQ:FQDQ + 2048].rearrange("p (h t) -> p h t", t=128),
                    FA[0:64, FQ64:FQ64 + 4096].rearrange("p (h t) -> p h t", t=GT)[:, :, chx * 128:chx * 128 + 128],
                    QDEC[:, :].rearrange("p (h t) -> p h t", t=128), ALU.mult))
                kq = s.done("vector", "V")
                s.wait("tensor", "V", kq)
                for r in range(2):
                    for j in range(8):
                        hh = r * 8 + j
                        gkv = hh // 4
                        bank = 4 + (j // 4)
                        pe(lambda E, hh=hh, gkv=gkv, bank=bank, j=j, chx=chx: E.matmul(
                            PS[bank][:, (j % 4) * 128:(j % 4) * 128 + 128],
                            FA[0:64, FK64 + gkv * GT + chx * 128:FK64 + gkv * GT + chx * 128 + 128],
                            FA[0:64, FQ64 + hh * GT + chx * 128:FQ64 + hh * GT + chx * 128 + 128],
                            start=True, stop=True))
                    td = s.done("tensor", "T")
                    s.wait("vector", "T", td)
                    for b in range(2):
                        v(lambda E, r=r, b=b: E.tensor_tensor(
                            FA[:, FSTS + r * 1024 + b * 512:FSTS + r * 1024 + (b + 1) * 512],
                            PS[4 + b][:, 0:512],
                            FA[:, FMASK + (r * 8 + b * 4) * 128:FMASK + (r * 8 + b * 4 + 4) * 128], ALU.mult))
                    std = s.done("vector", "V")
                    s.wait("tensor", "V", std)
                    for j in range(8):
                        hh = r * 8 + j
                        gkv = hh // 4
                        ob = 6 if j < 4 else 7
                        pe(lambda E, hh=hh, gkv=gkv, ob=ob, j=j, chx=chx: E.matmul(
                            PS[ob][0:64, (j % 4) * 128:(j % 4) * 128 + 128],
                            VTM[:, chx * 256 + gkv * 64:chx * 256 + gkv * 64 + 64],
                            FA[:, FSTS + hh * 128:FSTS + (hh + 1) * 128],
                            start=True, stop=False))
                        pe(lambda E, hh=hh, ob=ob, j=j: E.matmul(
                            PS[ob][0:64, (j % 4) * 128:(j % 4) * 128 + 128],
                            KVS[0:64, hh * 64:(hh + 1) * 64],
                            FA[0:64, FQDQ + hh * 128:FQDQ + (hh + 1) * 128],
                            start=False, stop=True))
                    for j in range(8):
                        hh = r * 8 + j
                        gkv = hh // 4
                        pe(lambda E, hh=hh, gkv=gkv, j=j, chx=chx: E.matmul(
                            PS[3][0:64, j * 64:(j + 1) * 64],
                            KDK[:, hh * 64:(hh + 1) * 64],
                            VTM[:, chx * 256 + gkv * 64:chx * 256 + gkv * 64 + 64],
                            start=True, stop=True))
                    td = s.done("tensor", "T")
                    s.wait("vector", "T", td)
                    for j in range(8):
                        hh = r * 8 + j
                        bd = float(_tab()[hh][128])
                        v(lambda E, hh=hh, j=j, bd=bd, r=r: E.scalar_tensor_tensor(
                            KVS[:, hh * 64:(hh + 1) * 64], KVS[:, hh * 64:(hh + 1) * 64],
                            bd, PS[3][0:64, j * 64:(j + 1) * 64], ALU.mult, ALU.add))
                    kvd = s.done("vector", "V")
                    s.wait("scalar", "T", td)
                    for j in range(8):
                        hh = r * 8 + j
                        cr = hh // 2
                        po = 64 * (hh % 2)
                        b = 6 if j < 4 else 7
                        sc(lambda E, b=b, j=j, cr=cr, po=po, chx=chx: E.activation(
                            B16[po:po + 64, OFH + cr * 256 + chx * 128:OFH + cr * 256 + chx * 128 + 128],
                            PS[b][0:64, (j % 4) * 128:(j % 4) * 128 + 128], ACT.Copy))
                    hid = s.done("scalar", "C")
                    s.wait("vector", "C", hid)
                    for j in range(8):
                        hh = r * 8 + j
                        cr = hh // 2
                        po = 64 * (hh % 2)
                        b = 6 if j < 4 else 7
                        v(lambda E, b=b, j=j, cr=cr, po=po, chx=chx: E.tensor_tensor(
                            B16[po:po + 64, OFL + cr * 256 + chx * 128:OFL + cr * 256 + chx * 128 + 128],
                            PS[b][0:64, (j % 4) * 128:(j % 4) * 128 + 128],
                            B16[po:po + 64, OFH + cr * 256 + chx * 128:OFH + cr * 256 + chx * 128 + 128],
                            ALU.subtract))
                    oev = s.done("vector", "V")
                    s.wait("tensor", "V", max(kvd, oev))

            for c in range(8):
                bank = c % 2
                s.wait("tensor", "V", wobank_vd[bank])
                mm3(PS[bank][:, 0:GT],
                    lambda kc, c=c: B16[:, WOH + kc * 1024 + c * 128:WOH + kc * 1024 + (c + 1) * 128],
                    lambda kc, c=c: B16[:, WOL + kc * 1024 + c * 128:WOL + kc * 1024 + (c + 1) * 128],
                    lambda kc: B16[:, OFH + kc * 256:OFH + (kc + 1) * 256],
                    lambda kc: B16[:, OFL + kc * 256:OFL + (kc + 1) * 256],
                    list(range(8)))
                td = s.done("tensor", "T")
                s.wait("vector", "T", td)
                v(lambda E, c=c, bank=bank, XH=XH: E.tensor_tensor(
                    FA[:, c * GT:(c + 1) * GT], PS[bank][:, 0:GT],
                    B16[:, XH + c * 256:XH + (c + 1) * 256], ALU.add))
                v(lambda E: E.drain())
                v(lambda E, c=c, XL=XL: E.tensor_tensor(
                    FA[:, c * GT:(c + 1) * GT], FA[:, c * GT:(c + 1) * GT],
                    B16[:, XL + c * 256:XL + (c + 1) * 256], ALU.add))
                vd = s.done("vector", "V")
                wobank_vd[bank] = vd
            s.wait("tensor", "V", vd)
            vd = emit_ln(0, RT, 1280, 0)
            s.wait("gpsimd", "V", vd)
            state["spill"] = s.dma("gpsimd",
                                   T["xpost"].rearrange("p (c t) -> p c t", c=8)[:, :, t0:t0 + GT],
                                   FA[:, 0:2048].rearrange("p (c t) -> p c t", c=8), "G")
            group_t_end[g] = s.cnt["T"]

        if "A" in phases:
            wq_t = tload(WQH, WQL, "wq", 1024, 1024, slab_done.get("wq", 0))
            wk_t = tload(WKH, WKL, "wk", 256, 1024, slab_done.get("wk", 0))
            wv_t = tload(WVH, WVL, "wv", 256, 1024, slab_done.get("wv", 0))
            wo_t = tload(WOH, WOL, "wo", 1024, 1024, slab_done.get("wo", 0))
            wA = max(wq_t, wk_t, wv_t, wo_t)
            v(lambda E: E.memset(KVS[:, :], 0.0))
            kvz = s.done("vector", "V")
            s.wait("tensor", "V", kvz)
            prefetch_group(0)
            for g in range(NG):
                emit_groupA(g, wA)

        xb_ready = [0, 0]
        b0_used = [0, 0]
        b0_rw_used = [0]
        b0bank_vd = {0: 0, 1: 0}

        def b0_load(bg):
            par = bg % 2
            base = FXB[par]
            t0 = bg * BT
            xpv = T["xpost"].rearrange("p (c t) -> p c t", c=8)
            s.wait("sync", "G", state["spill"])
            s.wait("sync", "V", b0_used[par])
            dst = FA[:, base:base + 8 * 516].rearrange("p (c t) -> p c t", t=516)
            if bg == 0:
                lv = s.dma("sync", dst[:, :, 2:2 + BT], xpv[:, :, 0:BT], "L")
            else:
                lv = s.dma("sync", dst[:, :, 0:BT + 2], xpv[:, :, t0 - 2:t0 + BT], "L")
            xb_ready[par] = lv

        def emit_b0(bg, pw_t):
            par = bg % 2
            base = FXB[par]
            t0 = bg * BT
            if bg + 1 < S // BT:
                b0_load(bg + 1)
            s.wait("vector", "L", max(xb_ready[par], consts_done))
            if bg == 0:
                v(lambda E, base=base: E.memset(
                    FA[:, base:base + 8 * 516].rearrange("p (c t) -> p c t", t=516)[:, :, 0:2], 0.0))
                v(lambda E: E.drain())
            dk = bias_col["dwk"]
            db = bias_col["dwb"]
            for c in range(8):
                xb = FA[:, base + c * 516:base + (c + 1) * 516]
                v(lambda E, xb=xb, c=c, dk=dk, db=db: E.tensor_scalar(
                    KDK[:, 512:1024], xb[:, 0:BT], SML[:, dk + c:dk + c + 1],
                    SML[:, db + c:db + c + 1], ALU.mult, ALU.add))
                v(lambda E: E.drain())
                v(lambda E, xb=xb, c=c, dk=dk: E.scalar_tensor_tensor(
                    KDK[:, 512:1024], xb[:, 1:BT + 1], SML[:, dk + 8 + c:dk + 8 + c + 1],
                    KDK[:, 512:1024], ALU.mult, ALU.add))
                v(lambda E: E.drain())
                v(lambda E, xb=xb, c=c, dk=dk: E.scalar_tensor_tensor(
                    KDK[:, 512:1024], xb[:, 2:BT + 2], SML[:, dk + 16 + c:dk + 16 + c + 1],
                    KDK[:, 512:1024], ALU.mult, ALU.add))
                v(lambda E: E.drain())
                v(lambda E, c=c: E.tensor_copy(B16[:, XCH + c * 512:XCH + (c + 1) * 512], KDK[:, 512:1024]))
                v(lambda E: E.drain())
                v(lambda E, c=c: E.tensor_tensor(B16[:, XCL + c * 512:XCL + (c + 1) * 512],
                                                 KDK[:, 512:1024],
                                                 B16[:, XCH + c * 512:XCH + (c + 1) * 512], ALU.subtract))
                v(lambda E: E.drain())
            vd = s.done("vector", "V")
            s.wait("tensor", "V", vd)
            s.wait("tensor", "L", pw_t)
            pb = bias_col["pwb"]
            for c in range(8):
                bank = c % 2
                s.wait("tensor", "V", b0bank_vd[bank])
                mm3(PS[bank][:, 0:BT],
                    lambda kc, c=c: B16[:, PWH + kc * 1024 + c * 128:PWH + kc * 1024 + (c + 1) * 128],
                    lambda kc, c=c: B16[:, PWL + kc * 1024 + c * 128:PWL + kc * 1024 + (c + 1) * 128],
                    lambda kc: B16[:, XCH + kc * 512:XCH + (kc + 1) * 512],
                    lambda kc: B16[:, XCL + kc * 512:XCL + (kc + 1) * 512],
                    list(range(8)))
                td = s.done("tensor", "T")
                s.wait("scalar", "T", td)
                sc(lambda E, bank=bank, c=c, pb=pb: E.activation(
                    KDK[:, bank * 512:bank * 512 + 512], PS[bank][:, 0:BT], ACT.Silu,
                    bias=SML[:, pb + c:pb + c + 1]))
                cd = s.done("scalar", "C")
                s.wait("vector", "C", cd)
                v(lambda E, base=base, c=c, bank=bank: E.tensor_tensor(
                    KDK[:, bank * 512:bank * 512 + 512],
                    FA[:, base + c * 516 + 2:base + c * 516 + 2 + BT],
                    KDK[:, bank * 512:bank * 512 + 512], ALU.add))
                v(lambda E: E.drain())
                v(lambda E, c=c, t0=t0, bank=bank: E.tensor_copy(
                    B16[:, X1H + c * 2048 + t0:X1H + c * 2048 + t0 + BT],
                    KDK[:, bank * 512:bank * 512 + 512]))
                v(lambda E: E.drain())
                v(lambda E, c=c, t0=t0, bank=bank: E.tensor_tensor(
                    B16[:, X1L + c * 2048 + t0:X1L + c * 2048 + t0 + BT],
                    KDK[:, bank * 512:bank * 512 + 512],
                    B16[:, X1H + c * 2048 + t0:X1H + c * 2048 + t0 + BT], ALU.subtract))
                vd = s.done("vector", "V")
                b0bank_vd[bank] = vd
            s.wait("tensor", "V", vd)
            b0_used[par] = vd
            for kc in range(8):
                pe(lambda E, kc=kc, t0=t0: E.matmul(
                    PS[2][0:8, 0:BT], RTW[:, kc * 8:(kc + 1) * 8],
                    B16[:, X1H + kc * 2048 + t0:X1H + kc * 2048 + t0 + BT],
                    start=(kc == 0), stop=False))
                pe(lambda E, kc=kc, t0=t0: E.matmul(
                    PS[2][0:8, 0:BT], RTW[:, 64 + kc * 8:64 + (kc + 1) * 8],
                    B16[:, X1H + kc * 2048 + t0:X1H + kc * 2048 + t0 + BT],
                    start=False, stop=False))
                pe(lambda E, kc=kc, t0=t0: E.matmul(
                    PS[2][0:8, 0:BT], RTW[:, kc * 8:(kc + 1) * 8],
                    B16[:, X1L + kc * 2048 + t0:X1L + kc * 2048 + t0 + BT],
                    start=False, stop=(kc == 7)))
            td = s.done("tensor", "T")
            s.wait("vector", "T", td)
            rb = bias_col["rwb"]
            v(lambda E, rb=rb: E.tensor_scalar(KDK[0:8, 512:1024], PS[2][0:8, 0:BT],
                                               SML[0:8, rb:rb + 1], None, ALU.add))
            vd = s.done("vector", "V")
            s.wait("tensor", "V", vd)
            for sbx in range(4):
                pe(lambda E, sbx=sbx: E.transpose(
                    PS[2][:, 0:8], KDK[0:8, 512 + sbx * 128:512 + (sbx + 1) * 128], ID32[0:8, 0:8]))
                td = s.done("tensor", "T")
                s.wait("vector", "T", td)
                v(lambda E: E.tensor_copy(LGS[:, 0:8], PS[2][:, 0:8]))
                v(lambda E: E.drain())
                v(lambda E: E.max(LGS[:, 16:24], LGS[:, 0:8]))
                v(lambda E: E.drain())
                v(lambda E: E.tensor_single_scalar(LGS[:, 24:32], LGS[:, 0:8], LGS[:, 16:17], ALU.subtract))
                v(lambda E: E.tensor_single_scalar(LGS[:, 32:40], LGS[:, 0:8], LGS[:, 17:18], ALU.is_ge))
                vd = s.done("vector", "V")
                s.wait("scalar", "V", vd)
                sc(lambda E: E.activation(LGS[:, 40:48], LGS[:, 24:32], ACT.Exp))
                cd = s.done("scalar", "C")
                s.wait("vector", "C", cd)
                v(lambda E: E.memset(LGS[:, 56:64], 1.0))
                v(lambda E: E.tensor_tensor(LGS[:, 48:56], LGS[:, 40:48], LGS[:, 32:40], ALU.mult))
                v(lambda E: E.drain())
                v(lambda E: E.tensor_reduce(LGS[:, 56:57], LGS[:, 48:56], AXX, ALU.add))
                v(lambda E: E.drain())
                v(lambda E: E.reciprocal(LGS[:, 40:48], LGS[:, 56:64]))
                v(lambda E: E.drain())
                v(lambda E: E.tensor_single_scalar(LGS[:, 8:16], LGS[:, 48:56], LGS[:, 40:41], ALU.mult))
                vd = s.done("vector", "V")
                s.wait("tensor", "V", vd)
                pe(lambda E: E.transpose(PS[2][0:8, 128:256], LGS[:, 8:16], ID32[:, :]))
                td = s.done("tensor", "T")
                s.wait("vector", "T", td)
                if sbx == 0:
                    s.wait("vector", "G", b0_rw_used[0])
                v(lambda E, sbx=sbx: E.tensor_copy(
                    CS[0:8, sbx * 128:(sbx + 1) * 128], PS[2][0:8, 128:256]))
                vd = s.done("vector", "V")
                s.wait("tensor", "V", vd)
            s.wait("gpsimd", "V", vd)
            b0_rw_used[0] = s.dma("gpsimd", T["rw_dram"][0:8, t0:t0 + BT], CS[0:8, 0:512], "G")

        if "B" in phases:
            pw_t = tload(PWH, PWL, "pww", 1024, 1024, slab_done.get("pww", 0),
                         after_t=s.cnt["T"])
            s.wait("sync", "G", slab_done.get("rww", 0))
            rwv_h = T["rww_hi"].rearrange("e (kc p) -> p kc e", p=128)
            rwv_l = T["rww_lo"].rearrange("e (kc p) -> p kc e", p=128)
            rl = 0
            for kc in range(8):
                s.dma("sync", RTW[:, kc * 8:(kc + 1) * 8], rwv_h[:, kc, :], "L", slow=True)
                rl = s.dma("sync", RTW[:, 64 + kc * 8:64 + (kc + 1) * 8], rwv_l[:, kc, :], "L", slow=True)
            s.wait("tensor", "L", rl)
            b0_load(0)
            for bg in range(S // BT):
                emit_b0(bg, pw_t)
            state["x1_done"] = s.cnt["V"]

        ew_t = [0, 0]
        ew_free = [0, 0]
        crw_used = [0]
        cbank_vd = {0: 0, 1: 0}

        def c_prefetch(e, oh, par, gate_t):
            s.wait("sync", "G", slab_done.get("expw", 0))
            s.wait("sync", "V", ew_free[par])
            s.wait("sync", "T", gate_t)
            base = EWP[par]
            lv = 0
            r0 = e * 1024 + oh * 512
            for kc in range(8):
                lv = s.dma("sync", B16[:, base + kc * 512:base + (kc + 1) * 512],
                           T["expw_hi"][r0:r0 + 512, kc * 128:(kc + 1) * 128], "L", transpose=True)
                lv = s.dma("sync", B16[:, base + 4096 + kc * 512:base + 4096 + (kc + 1) * 512],
                           T["expw_lo"][r0:r0 + 512, kc * 128:(kc + 1) * 128], "L", transpose=True)
            ew_t[par] = lv

        def emit_cexp(sg0, e, oh, par):
            eb = bias_col["expb"]
            base = EWP[par]
            s.wait("tensor", "L", ew_t[par])
            for th in range(2):
                tb = sg0 + th * 512
                s.wait("sync", "V", s.cnt["V"])
                s.wait("sync", "G", s.cnt["G"])
                lv = s.dma("sync", CS[0:8, 0:512], T["rw_dram"][0:8, tb:tb + 512], "L")
                s.wait("tensor", "L", lv)
                pe(lambda E: E.matmul(PS[2][0:1, 0:512], ID32[0:8, e:e + 1],
                                      CS[0:8, 0:512], start=True, stop=True))
                td = s.done("tensor", "T")
                s.wait("vector", "T", td)
                v(lambda E: E.tensor_copy(ROWS[0:1, 0:512], PS[2][0:1, 0:512]))
                vd = s.done("vector", "V")
                crw_used[0] = vd
                s.wait("tensor", "V", vd)
                pe(lambda E: E.matmul(PS[2][:, 0:512], ONES1[:, :], ROWS[0:1, 0:512], start=True, stop=True))
                td = s.done("tensor", "T")
                s.wait("vector", "T", td)
                for oc in range(4):
                    co = oh * 4 + oc
                    bank = oc % 2
                    s.wait("tensor", "V", cbank_vd[bank])
                    mm3(PS[bank][:, 0:512],
                        lambda kc, base=base, oc=oc: B16[:, base + kc * 512 + oc * 128:base + kc * 512 + oc * 128 + 128],
                        lambda kc, base=base, oc=oc: B16[:, base + 4096 + kc * 512 + oc * 128:base + 4096 + kc * 512 + oc * 128 + 128],
                        lambda kc, tb=tb: B16[:, X1H + kc * 2048 + tb:X1H + kc * 2048 + tb + 512],
                        lambda kc, tb=tb: B16[:, X1L + kc * 2048 + tb:X1L + kc * 2048 + tb + 512],
                        list(range(8)))
                    td = s.done("tensor", "T")
                    s.wait("scalar", "T", td)
                    sc(lambda E, bank=bank, e=e, co=co, eb=eb: E.activation(
                        FA[:, FSCR + bank * 512:FSCR + bank * 512 + 512], PS[bank][:, 0:512], ACT.Silu,
                        bias=SML[:, eb + e * 8 + co:eb + e * 8 + co + 1]))
                    cd = s.done("scalar", "C")
                    s.wait("vector", "C", cd)
                    dst_off = FCMB + co * 1024 + th * 512
                    if e == 0:
                        v(lambda E, dst_off=dst_off, bank=bank: E.tensor_tensor(
                            FA[:, dst_off:dst_off + 512], FA[:, FSCR + bank * 512:FSCR + bank * 512 + 512],
                            PS[2][:, 0:512], ALU.mult))
                    else:
                        v(lambda E, bank=bank: E.tensor_tensor(
                            FA[:, 9216:9728], FA[:, FSCR + bank * 512:FSCR + bank * 512 + 512],
                            PS[2][:, 0:512], ALU.mult))
                        v(lambda E: E.drain())
                        v(lambda E, dst_off=dst_off: E.tensor_tensor(
                            FA[:, dst_off:dst_off + 512], FA[:, dst_off:dst_off + 512],
                            FA[:, 9216:9728], ALU.add))
                    vd = s.done("vector", "V")
                    cbank_vd[bank] = vd
            ew_free[par] = s.cnt["V"]

        xp_used = [0, 0]
        dbank_vd = {0: 0, 1: 0}
        fubank_vd = {0: 0, 1: 0}

        def emit_dgroup(sg0, dg):
            t0 = sg0 + dg * GT
            toff = (sg0 % SGT) + dg * GT
            toff = dg * GT
            ob = bias_col["outb"]
            nw = bias_col["normw"]
            fb = bias_col["ffdb"]
            ub = bias_col["ffub"]
            for co in range(8):
                v(lambda E, co=co, toff=toff: E.tensor_copy(
                    B16[:, CBH + co * 256:CBH + (co + 1) * 256],
                    FA[:, FCMB + co * 1024 + toff:FCMB + co * 1024 + toff + GT]))
                v(lambda E: E.drain())
                v(lambda E, co=co, toff=toff: E.tensor_tensor(
                    B16[:, CBL + co * 256:CBL + (co + 1) * 256],
                    FA[:, FCMB + co * 1024 + toff:FCMB + co * 1024 + toff + GT],
                    B16[:, CBH + co * 256:CBH + (co + 1) * 256], ALU.subtract))
                v(lambda E: E.drain())
            vd = s.done("vector", "V")
            s.wait("tensor", "V", vd)
            for c in range(8):
                bank = c % 2
                s.wait("tensor", "V", dbank_vd[bank])
                mm3(PS[bank][:, 0:GT],
                    lambda kc, c=c: B16[:, OWH + kc * 1024 + c * 128:OWH + kc * 1024 + (c + 1) * 128],
                    lambda kc, c=c: B16[:, OWL + kc * 1024 + c * 128:OWL + kc * 1024 + (c + 1) * 128],
                    lambda kc: B16[:, CBH + kc * 256:CBH + (kc + 1) * 256],
                    lambda kc: B16[:, CBL + kc * 256:CBL + (kc + 1) * 256],
                    list(range(8)))
                td = s.done("tensor", "T")
                s.wait("vector", "T", td)
                v(lambda E, c=c, bank=bank, ob=ob: E.tensor_scalar(
                    CS[:, 0:GT], PS[bank][:, 0:GT], SML[:, ob + c:ob + c + 1], None, ALU.add))
                v(lambda E: E.drain())
                v(lambda E, c=c, t0=t0: E.tensor_tensor(
                    CS[:, 0:GT], CS[:, 0:GT], B16[:, X1H + c * 2048 + t0:X1H + c * 2048 + t0 + GT], ALU.add))
                v(lambda E: E.drain())
                v(lambda E, c=c, t0=t0: E.tensor_tensor(
                    CS[:, 0:GT], CS[:, 0:GT], B16[:, X1L + c * 2048 + t0:X1L + c * 2048 + t0 + GT], ALU.add))
                v(lambda E: E.drain())
                v(lambda E, c=c: E.tensor_copy(FA[:, FXAD + c * GT:FXAD + (c + 1) * GT], CS[:, 0:GT]))
                vd = s.done("vector", "V")
                dbank_vd[bank] = vd
            s.wait("tensor", "V", vd)
            vd = s.done("vector", "V")
            s.wait("scalar", "V", vd)
            for c in range(8):
                sc(lambda E, c=c: E.activation(CS[:, 256:256 + GT],
                                               FA[:, FXAD + c * GT:FXAD + (c + 1) * GT], ACT.Square))
                cd = s.done("scalar", "C")
                s.wait("tensor", "C", cd)
                pe(lambda E, c=c: E.matmul(PS[4][0:1, 0:GT], ONESC[:, :], CS[:, 256:256 + GT],
                                           start=(c == 0), stop=(c == 7)))
                td = s.done("tensor", "T")
                s.wait("scalar", "T", td)
            td = s.done("tensor", "T")
            s.wait("vector", "T", td)
            v(lambda E: E.tensor_copy(ROWS[0:1, 0:GT], PS[4][0:1, 0:GT]))
            v(lambda E: E.drain())
            v(lambda E: E.tensor_scalar(ROWS[0:1, 0:GT], ROWS[0:1, 0:GT], 1.0 / D, None, ALU.mult))
            vd = s.done("vector", "V")
            s.wait("scalar", "V", vd)
            sc(lambda E: E.activation(ROWS[0:1, 0:GT], ROWS[0:1, 0:GT], ACT.Sqrt, bias=EPSC[0:1, 1:2]))
            cd = s.done("scalar", "C")
            s.wait("vector", "C", cd)
            v(lambda E: E.reciprocal(ROWS[0:1, 256:256 + GT], ROWS[0:1, 0:GT]))
            vd = s.done("vector", "V")
            s.wait("tensor", "V", vd)
            pe(lambda E: E.matmul(PS[4][:, 0:GT], ONES1[:, :], ROWS[0:1, 256:256 + GT], start=True, stop=True))
            td = s.done("tensor", "T")
            s.wait("vector", "T", td)
            for c in range(8):
                v(lambda E, c=c: E.tensor_tensor(
                    FA[:, FXAD + c * GT:FXAD + (c + 1) * GT],
                    FA[:, FXAD + c * GT:FXAD + (c + 1) * GT], PS[4][:, 0:GT], ALU.mult))
                v(lambda E: E.drain())
                v(lambda E, c=c, nw=nw: E.tensor_scalar(
                    FA[:, FXAD + c * GT:FXAD + (c + 1) * GT],
                    FA[:, FXAD + c * GT:FXAD + (c + 1) * GT], SML[:, nw + c:nw + c + 1], None, ALU.mult))
                v(lambda E: E.drain())
                v(lambda E, c=c: E.tensor_copy(B16[:, HHI + c * 256:HHI + (c + 1) * 256],
                                               FA[:, FXAD + c * GT:FXAD + (c + 1) * GT]))
                v(lambda E: E.drain())
                v(lambda E, c=c: E.tensor_tensor(B16[:, HLO + c * 256:HLO + (c + 1) * 256],
                                                 FA[:, FXAD + c * GT:FXAD + (c + 1) * GT],
                                                 B16[:, HHI + c * 256:HHI + (c + 1) * 256], ALU.subtract))
                v(lambda E: E.drain())
            vd = s.done("vector", "V")
            s.wait("tensor", "V", vd)
            for c2 in range(2):
                bank = c2 % 2
                mm3(PS[bank][:, 0:GT],
                    lambda kc, c2=c2: B16[:, FDH + kc * 256 + c2 * 128:FDH + kc * 256 + c2 * 128 + 128],
                    lambda kc, c2=c2: B16[:, FDL + kc * 256 + c2 * 128:FDL + kc * 256 + c2 * 128 + 128],
                    lambda kc: B16[:, HHI + kc * 256:HHI + (kc + 1) * 256],
                    lambda kc: B16[:, HLO + kc * 256:HLO + (kc + 1) * 256],
                    list(range(8)))
                td = s.done("tensor", "T")
                s.wait("scalar", "T", td)
                sc(lambda E, bank=bank, c2=c2, fb=fb: E.activation(
                    CS[:, 512 + c2 * 256:512 + (c2 + 1) * 256], PS[bank][:, 0:GT], ACT.Gelu,
                    bias=SML[:, fb + c2:fb + c2 + 1]))
                cd = s.done("scalar", "C")
                s.wait("vector", "C", cd)
                v(lambda E, c2=c2: E.tensor_copy(GHL[:, c2 * 256:(c2 + 1) * 256],
                                                 CS[:, 512 + c2 * 256:512 + (c2 + 1) * 256]))
                v(lambda E: E.drain())
                v(lambda E, c2=c2: E.tensor_tensor(GHL[:, 512 + c2 * 256:512 + (c2 + 1) * 256],
                                                   CS[:, 512 + c2 * 256:512 + (c2 + 1) * 256],
                                                   GHL[:, c2 * 256:(c2 + 1) * 256], ALU.subtract))
                vd = s.done("vector", "V")
                s.wait("tensor", "V", vd)
            for c in range(8):
                bank = c % 2
                dpar = c % 2
                s.wait("tensor", "V", fubank_vd[bank])
                s.wait("sync", "G", state["spill"])
                s.wait("sync", "V", xp_used[dpar])
                xpl = s.dma("sync", FA[:, FXPJ + dpar * 256:FXPJ + dpar * 256 + 256],
                            T["xpost"][:, c * 2048 + t0:c * 2048 + t0 + GT], "L")
                mm3(PS[bank][:, 0:GT],
                    lambda kc, c=c: B16[:, FUH + kc * 1024 + c * 128:FUH + kc * 1024 + (c + 1) * 128],
                    lambda kc, c=c: B16[:, FUL + kc * 1024 + c * 128:FUL + kc * 1024 + (c + 1) * 128],
                    lambda kc: GHL[:, kc * 256:(kc + 1) * 256],
                    lambda kc: GHL[:, 512 + kc * 256:512 + (kc + 1) * 256],
                    list(range(2)))
                td = s.done("tensor", "T")
                s.wait("vector", "T", td)
                s.wait("vector", "L", xpl)
                v(lambda E, c=c, bank=bank, ub=ub: E.tensor_scalar(
                    CS[:, 0:GT], PS[bank][:, 0:GT], SML[:, ub + c:ub + c + 1], None, ALU.add))
                v(lambda E: E.drain())
                v(lambda E, c=c: E.tensor_tensor(CS[:, 0:GT], CS[:, 0:GT],
                                                 B16[:, HHI + c * 256:HHI + (c + 1) * 256], ALU.add))
                v(lambda E: E.drain())
                v(lambda E, c=c: E.tensor_tensor(CS[:, 0:GT], CS[:, 0:GT],
                                                 B16[:, HLO + c * 256:HLO + (c + 1) * 256], ALU.add))
                v(lambda E: E.drain())
                v(lambda E, c=c, dpar=dpar: E.tensor_tensor(
                    FA[:, FXAD + c * GT:FXAD + (c + 1) * GT], CS[:, 0:GT],
                    FA[:, FXPJ + dpar * 256:FXPJ + dpar * 256 + 256], ALU.add))
                vd = s.done("vector", "V")
                fubank_vd[bank] = vd
                xp_used[dpar] = vd
            s.wait("tensor", "V", vd)
            emit_ln(FXAD, CS, 256, 0)
            for th in range(2):
                for half in range(2):
                    for cq in range(4):
                        c = half * 4 + cq
                        pe(lambda E, c=c, cq=cq, th=th: E.transpose(
                            PS[6][:, cq * 128:(cq + 1) * 128],
                            FA[:, FXAD + c * GT + th * 128:FXAD + c * GT + th * 128 + 128],
                            ID32[:, :]))
                    td = s.done("tensor", "T")
                    s.wait("vector", "T", td)
                    v(lambda E: E.tensor_copy(CS[:, 0:512], PS[6][:, 0:512]))
                    vd = s.done("vector", "V")
                    s.wait("tensor", "V", vd)
                    s.wait("gpsimd", "V", vd)
                    gv = s.dma("gpsimd", T["out"][t0 + th * 128:t0 + th * 128 + 128,
                                                  half * 512:half * 512 + 512], CS[:, 0:512], "G")
                    s.wait("vector", "G", gv)

        if "C" in phases:
            for sg in range(S // SGT):
                sg0 = sg * SGT
                gate_t = s.cnt["T"]
                c_prefetch(0, 0, 0, gate_t)
                seq = [(e, oh) for e in range(8) for oh in range(2)]
                for idx, (e, oh) in enumerate(seq):
                    par = idx % 2
                    if idx + 1 < len(seq):
                        c_prefetch(seq[idx + 1][0], seq[idx + 1][1], par ^ 1, gate_t)
                    if sg == 0 and idx == 0:
                        s.wait("tensor", "V", state["x1_done"])
                    emit_cexp(sg0, e, oh, par)
                if "D" in phases:
                    dgate = s.cnt["T"]
                    ow_t = tload(OWH, OWL, "outw", 1024, 1024, slab_done.get("outw", 0),
                                 after_t=dgate)
                    fd_t = tload(FDH, FDL, "ffdw", 256, 1024, slab_done.get("ffdw", 0),
                                 after_t=dgate)
                    fu_t = tload(FUH, FUL, "ffuw", 1024, 256, slab_done.get("ffuw", 0),
                                 after_t=dgate)
                    s.wait("tensor", "L", max(ow_t, fd_t, fu_t))
                    for dg in range(SGT // GT):
                        emit_dgroup(sg0, dg)

        if taps:
            vd = s.done("vector", "V")
            cd = s.done("scalar", "C")
            td = s.done("tensor", "T")
            pd = s.done("gpsimd", "P")
            s.wait("gpsimd", "V", vd)
            s.wait("gpsimd", "C", cd)
            s.wait("gpsimd", "T", td)
            for (tn, ap_fn, shp, dt) in taps:
                s.dma("gpsimd", tap_dram[tn], ap_fn(T), "G")

        s.wait("gpsimd", "G", s.cnt["G"])

        with nc.Block() as block:
            @block.sync
            def _(sync):
                for fn in s.ops["sync"]:
                    fn(sync)

            @block.tensor
            def _(tensor):
                for fn in s.ops["tensor"]:
                    fn(tensor)

            @block.vector
            def _(vector):
                for fn in s.ops["vector"]:
                    fn(vector)

            @block.scalar
            def _(scalar):
                for fn in s.ops["scalar"]:
                    fn(scalar)

            @block.gpsimd
            def _(gpsimd):
                for fn in s.ops["gpsimd"]:
                    fn(gpsimd)
    return nc


def kernel(L0_attn_x, L0_attn_Wq, L0_attn_Wk, L0_attn_Wv, L0_attn_Wo,
           L0_ffn_dw_kernel, L0_ffn_dw_bias, L0_ffn_pw_w, L0_ffn_pw_b,
           L0_ffn_router_w, L0_ffn_router_b, L0_ffn_expert_w, L0_ffn_expert_b,
           L0_ffn_out_w, L0_ffn_out_b, L0_ffn_norm_w, L0_ffn_ff_down_w,
           L0_ffn_ff_down_b, L0_ffn_ff_up_w, L0_ffn_ff_up_b):
    consts = host_consts()
    nc = build_program()
    x = np.ascontiguousarray(np.asarray(L0_attn_x, dtype=np.float32))
    weights = {
        "wq": L0_attn_Wq, "wk": L0_attn_Wk, "wv": L0_attn_Wv, "wo": L0_attn_Wo,
        "dwk": np.asarray(L0_ffn_dw_kernel, dtype=np.float32).reshape(D, 3),
        "dwb": L0_ffn_dw_bias, "pww": L0_ffn_pw_w, "pwb": L0_ffn_pw_b,
        "rww": L0_ffn_router_w, "rwb": L0_ffn_router_b,
        "expw": np.asarray(L0_ffn_expert_w, dtype=np.float32).reshape(8 * D, D),
        "expb": L0_ffn_expert_b, "outw": L0_ffn_out_w, "outb": L0_ffn_out_b,
        "normw": L0_ffn_norm_w, "ffdw": L0_ffn_ff_down_w, "ffdb": L0_ffn_ff_down_b,
        "ffuw": L0_ffn_ff_up_w, "ffub": L0_ffn_ff_up_b,
    }
    weights = {k: np.ascontiguousarray(np.asarray(vv, dtype=np.float32)) for k, vv in weights.items()}
    in_maps = []
    for b in range(8):
        m = {"x": np.ascontiguousarray(x[b])}
        m.update(weights)
        m.update(consts)
        in_maps.append(m)
    res = run_bass_kernel_spmd(nc, in_maps, core_ids=list(range(8)))
    return np.stack([np.asarray(r["out"]) for r in res.results], axis=0)
